# version 13
# baseline (speedup 1.0000x reference)
import contextlib
import numpy as np
import ml_dtypes
import concourse.bass as bass
import concourse.mybir as mybir
from concourse.bass_utils import run_bass_kernel_spmd

F32 = mybir.dt.float32
BF16 = mybir.dt.bfloat16
AF = mybir.ActivationFunctionType
ALU = mybir.AluOpType
AX = mybir.AxisListType

P = 128
HID = 2048
T = 2048
FC = 16
TC = 16
HALF = 2
TCH = 8
TTH = 1024
KV = 512
KVH = 4
H = 16
LN_EPS = 1e-5
ATTN_EPS = 1e-6
NTG = 2
TTG = 1024
NR = 4
FG = 8
INV_N = 1.0 / HID

DEBUG_PHASE = 0
DEBUG_NO_CC = False


class Sem:
    def __init__(self, stack, nc, name):
        self.h = stack.enter_context(nc.semaphore(name))
        self.n = 0

    def inc(self, ins, k=1):
        ins.then_inc(self.h, k)
        self.n += k
        return ins

    def barrier(self, eng):
        eng.wait_ge(self.h, self.n)


def build_nc(debug=False):
    st = contextlib.ExitStack()
    nc = bass.Bass("TRN2", target_bir_lowering=False)

    x_in = nc.dram_tensor("x_in", [T, HID], F32, kind="ExternalInput").ap()
    wkv_blk = nc.dram_tensor("wkv_blk", [P, FC * 2 * KV], F32, kind="ExternalInput").ap()
    wq_blk = nc.dram_tensor("wq_blk", [8, P, FC * 256], F32, kind="ExternalInput").ap()
    wo_blk = nc.dram_tensor("wo_blk", [4, P, FC * 512], F32, kind="ExternalInput").ap()
    wgu_blk = nc.dram_tensor("wgu_blk", [64, P, FC * 256], F32, kind="ExternalInput").ap()
    wd_blk = nc.dram_tensor("wd_blk", [NR * FG, P, FC * 256], F32, kind="ExternalInput").ap()
    bq_t = nc.dram_tensor("bq_t", [P, FC], F32, kind="ExternalInput").ap()
    bkv_r = nc.dram_tensor("bkv_r", [1, 2 * KV], F32, kind="ExternalInput").ap()
    id_bf = nc.dram_tensor("id_bf", [P, P], BF16, kind="ExternalInput").ap()
    id_f32 = nc.dram_tensor("id_f32", [P, P], F32, kind="ExternalInput").ap()
    sel8_d = nc.dram_tensor("sel8_d", [8, 1024], BF16, kind="ExternalInput").ap()
    out = nc.dram_tensor("out", [T, HID], F32, kind="ExternalOutput").ap()

    x1_sp = nc.dram_tensor("x1_sp", [T, HID], F32).ap()
    xn2_sp = nc.dram_tensor("xn2_sp", [T, HID], BF16).ap()
    cc_in_t = nc.dram_tensor("cc_in", [P, KVH * 130], BF16)
    cc_out_t = nc.dram_tensor("cc_out", [P, KVH * 130], BF16)

    dbg = {}
    if debug:
        dbg["d_sq"] = nc.dram_tensor("d_sq", [P, H * T], BF16, kind="ExternalOutput").ap()
        dbg["d_kvf"] = nc.dram_tensor("d_kvf", [P, KVH * 130], F32, kind="ExternalOutput").ap()
        if DEBUG_PHASE != 1:
            dbg["d_x1"] = nc.dram_tensor("d_x1", [T, HID], F32, kind="ExternalOutput").ap()
            dbg["d_xn2"] = nc.dram_tensor("d_xn2", [T, HID], BF16, kind="ExternalOutput").ap()

    Sm = lambda name: Sem(st, nc, name)
    E = {}
    for name in ["s_c", "s_x0", "s_x1", "s_wkv", "s_wq0", "s_wq1", "s_wo",
                 "s_wo1", "s_wo2", "s_wo3", "s_wq2", "s_wgu0", "s_wgu1", "s_wd0", "s_wd1", "s_cc",
                 "s_x30", "s_x31", "s_spx0", "s_spx1", "s_spn0", "s_spn1",
                 "s_x20", "s_x21", "s_x1r0", "s_x1r1", "s_out0", "s_out1", "c_bias",
                 "c_red", "c_sq", "c_var", "c_rstd", "c_negmr", "c_ln",
                 "c_tr", "c_xn1", "c_kvp", "c_pk", "c_e", "c_skv", "c_ka",
                 "c_ccp", "c_kvbf", "c_qpe", "c_qact", "c_qev",
                 "c_n3a", "c_nact", "c_b3a", "c_inv", "c_q3a", "c_attn",
                 "c_ope", "c_x1e", "c2_red", "c2_sq", "c2_var", "c2_rstd",
                 "c2_negmr", "c_ln2", "c_x2t", "c_x2f",
                 "c_gpe", "c_upe", "c_sil", "c_guev",
                 "c_dpe", "c_dev", "c_dwd", "c_ddone", "c_ft", "c_fev", "c_fin",
                 "c_dv", "c_ac", "c_rowp", "c_row", "c_qb3", "c_qbc", "c_stg", "c_sqp", "c_sqs", "c_qln"]:
        E[name] = Sm(name)
    E["s_wol"] = [E["s_wo"], E["s_wo1"], E["s_wo2"], E["s_wo3"]]
    E["s_x"] = [E["s_x0"], E["s_x1"]]
    E["s_wq"] = [E["s_wq0"], E["s_wq1"], E["s_wq2"]]
    E["s_wgu"] = [E["s_wgu0"], E["s_wgu1"]]
    E["s_wd"] = [E["s_wd0"], E["s_wd1"]]
    E["s_x3"] = [E["s_x30"], E["s_x31"]]
    E["s_spx"] = [E["s_spx0"], E["s_spx1"]]
    E["s_spn"] = [E["s_spn0"], E["s_spn1"]]
    E["s_x2"] = [E["s_x20"], E["s_x21"]]
    E["s_x1r"] = [E["s_x1r0"], E["s_x1r1"]]
    E["s_out"] = [E["s_out0"], E["s_out1"]]
    E.update(dict(x_in=x_in, wkv_blk=wkv_blk, wq_blk=wq_blk, wo_blk=wo_blk,
                  wgu_blk=wgu_blk, wd_blk=wd_blk, bq_t=bq_t, bkv_r=bkv_r,
                  id_bf=id_bf, id_f32=id_f32, sel8_d=sel8_d, out=out,
                  x1_sp=x1_sp, xn2_sp=xn2_sp,
                  cc_in_t=cc_in_t, cc_out_t=cc_out_t, debug=debug, dbg=dbg))

    with (
        nc.sbuf_tensor("ident_bf", [P, P], BF16) as ident_bf,
        nc.sbuf_tensor("ident_f", [P, P], F32) as ident_f,
        nc.sbuf_tensor("ones_row_f", [1, P], F32) as ones_row_f,
        nc.sbuf_tensor("ones_col_bf", [P, 1], BF16) as ones_col_bf,
        nc.sbuf_tensor("ones_row_bf", [1, P], BF16) as ones_row_bf,
        nc.sbuf_tensor("sel8_sb", [8, 1024], BF16) as sel8_sb,
        nc.sbuf_tensor("bq_sb", [P, FC], F32) as bq_sb,
        nc.sbuf_tensor("negbq_sb", [P, FC], F32) as negbq_sb,
        nc.sbuf_tensor("bq1_sb", [P, FC], F32) as bq1_sb,
        nc.sbuf_tensor("bkv_sb", [1, 2 * KV], F32) as bkv_sb,
        nc.sbuf_tensor("bk1_bc", [P, KV], F32) as bk1_bc,
        nc.sbuf_tensor("bv_bc", [P, KV], F32) as bv_bc,
        nc.sbuf_tensor("stat", [P, 24], F32) as stat,
        nc.sbuf_tensor("eps_ln", [P, 1], F32) as eps_ln,
        nc.sbuf_tensor("eps_at", [P, 1], F32) as eps_at,
        nc.sbuf_tensor("kv_bf", [P, KVH * P], BF16) as kv_bf,
        nc.sbuf_tensor("ksum_bf", [P, KVH], BF16) as ksum_bf,
        nc.sbuf_tensor("kvf_sb", [P, KVH * 130], BF16) as kvf_sb,
        nc.sbuf_tensor("cc_sb", [P, KVH * 130], BF16) as cc_sb,
        nc.Block() as block,
    ):
        E.update(dict(ident_bf=ident_bf, ident_f=ident_f, ones_row_f=ones_row_f,
                      ones_col_bf=ones_col_bf, ones_row_bf=ones_row_bf, sel8_sb=sel8_sb,
                      bq_sb=bq_sb, negbq_sb=negbq_sb, bq1_sb=bq1_sb, bkv_sb=bkv_sb,
                      bk1_bc=bk1_bc, bv_bc=bv_bc, stat=stat, kv_bf=kv_bf,
                      eps_ln=eps_ln, eps_at=eps_at,
                      ksum_bf=ksum_bf, kvf_sb=kvf_sb, cc_sb=cc_sb))
        s_c = E["s_c"]; c_bias = E["c_bias"]; s_cc = E["s_cc"]

        @block.sync
        def _(sync):
            s_c.inc(sync.dma_start(out=ident_bf[:], in_=id_bf), 16)
            s_c.inc(sync.dma_start(out=ident_f[:], in_=id_f32), 16)
            s_c.inc(sync.dma_start(out=bq_sb[:], in_=bq_t), 16)
            s_c.inc(sync.dma_start(out=bkv_sb[:], in_=bkv_r), 16)
            s_c.inc(sync.dma_start(out=sel8_sb[:], in_=sel8_d), 16)

        @block.vector
        def _(v):
            v.wait_ge(s_c.h, 80)
            v.memset(ones_row_f[:], 1.0)
            v.memset(ones_col_bf[:], 1.0)
            v.memset(ones_row_bf[:], 1.0)
            v.memset(eps_ln[:], LN_EPS)
            v.memset(eps_at[:], ATTN_EPS)
            v.tensor_scalar(out=negbq_sb[:], in0=bq_sb[:], scalar1=-1.0, scalar2=None, op0=ALU.mult)
            c_bias.inc(v.tensor_scalar(out=bq1_sb[:], in0=bq_sb[:], scalar1=1.0, scalar2=None, op0=ALU.add))

        with nc.psum_tensor("ps_bias", [P, 2 * KV], F32) as ps_bias:
            @block.tensor
            def _(t):
                t.wait_ge(c_bias.h, 1)
                t.matmul(ps_bias[:, 0:KV], ones_row_f[:], bkv_sb[:, 0:KV], start=True, stop=True)
                c_bias.inc(t.matmul(ps_bias[:, KV:2 * KV], ones_row_f[:], bkv_sb[:, KV:2 * KV],
                                    start=True, stop=True))

            @block.vector
            def _(v):
                v.wait_ge(c_bias.h, 2)
                v.tensor_scalar(out=bk1_bc[:], in0=ps_bias[:, 0:KV], scalar1=1.0, scalar2=None, op0=ALU.add)
                c_bias.inc(v.tensor_copy(out=bv_bc[:], in_=ps_bias[:, KV:2 * KV]))

        with nc.sbuf_tensor("sq_sb", [P, H * T], BF16) as sq_sb:
            E["sq_sb"] = sq_sb
            _p1(nc, block, E)
            if DEBUG_PHASE == 1:
                @block.sync
                def _(sync):
                    so = E["s_out"][0]
                    sync.wait_ge(E["c_qev"].h, 64)
                    sync.wait_ge(E["c_kvbf"].h, 1)
                    so.inc(sync.dma_start(out=dbg["d_sq"], in_=sq_sb[:]), 16)
                    so.inc(sync.dma_start(out=dbg["d_kvf"], in_=kvf_sb[:]), 16)
                    sync.wait_ge(so.h, so.n)
                return nc, st
            _p3(nc, block, E)

        if DEBUG_PHASE == 3:
            @block.sync
            def _(sync):
                for s in (E["s_spx0"], E["s_spx1"], E["s_spn0"], E["s_spn1"]):
                    sync.wait_ge(s.h, s.n)
                if debug:
                    so = E["s_out0"]
                    so.inc(sync.dma_start(out=dbg["d_x1"], in_=x1_sp), 16)
                    so.inc(sync.dma_start(out=dbg["d_xn2"], in_=xn2_sp), 16)
                    sync.wait_ge(so.h, so.n)
            return nc, st

        _p4(nc, block, E)

        @block.sync
        def _(sync):
            sync.wait_ge(E["s_out0"].h, E["s_out0"].n)
            sync.wait_ge(E["s_out1"].h, E["s_out1"].n)

    return nc, st


def _p1(nc, block, E):
    x_in = E["x_in"]; sq_sb = E["sq_sb"]; stat = E["stat"]
    ident_bf = E["ident_bf"]; ident_f = E["ident_f"]; ones_row_f = E["ones_row_f"]
    ones_col_bf = E["ones_col_bf"]; sel8_sb = E["sel8_sb"]; ones_row_bf = E["ones_row_bf"]
    bk1_bc = E["bk1_bc"]; bv_bc = E["bv_bc"]; negbq_sb = E["negbq_sb"]; bq1_sb = E["bq1_sb"]
    wkv_blk = E["wkv_blk"]; wq_blk = E["wq_blk"]
    cc_sb = E["cc_sb"]; kvf_sb = E["kvf_sb"]; kv_bf = E["kv_bf"]; ksum_bf = E["ksum_bf"]
    cc_in_t = E["cc_in_t"]; cc_out_t = E["cc_out_t"]
    s_x = E["s_x"]; s_wkv = E["s_wkv"]; s_wq = E["s_wq"]; s_cc = E["s_cc"]
    c_bias = E["c_bias"]
    c_red = E["c_red"]; c_sq = E["c_sq"]; c_var = E["c_var"]; c_rstd = E["c_rstd"]
    c_negmr = E["c_negmr"]; c_ln = E["c_ln"]; c_tr = E["c_tr"]; c_xn1 = E["c_xn1"]
    c_kvp = E["c_kvp"]; c_pk = E["c_pk"]; c_e = E["c_e"]; c_skv = E["c_skv"]
    c_ka = E["c_ka"]; c_ccp = E["c_ccp"]; c_kvbf = E["c_kvbf"]
    c_qpe = E["c_qpe"]; c_qact = E["c_qact"]; c_qev = E["c_qev"]
    c_dv = E["c_dv"]; c_ac = E["c_ac"]
    c_rowp = E["c_rowp"]; c_row = E["c_row"]; c_qb3 = E["c_qb3"]; c_qbc = E["c_qbc"]
    c_stg = E["c_stg"]; c_sqp = E["c_sqp"]; c_sqs = E["c_sqs"]; c_qln = E["c_qln"]
    eps_ln = E["eps_ln"]

    p1 = contextlib.ExitStack()
    sb = lambda *a: p1.enter_context(nc.sbuf_tensor(*a))
    pt = lambda *a: p1.enter_context(nc.psum_tensor(*a))
    xn1_sb = sb("xn1_sb", [P, FC * TTH], BF16)
    wkv_sb = sb("wkv_sb", [P, FC * 2 * KV], BF16)
    wq_sb = sb("wq_sb", [P, 2 * FC * 256], BF16)
    x_sb = sb("x_sb", [P, 2 * HID], F32)
    xnt_sb = sb("xnt_sb", [P, 2 * HID], BF16)
    sk_sb = sb("sk_sb", [P, 2 * KV], BF16)
    v_sb = sb("v_sb", [P, 2 * KVH * 130], BF16)
    pk_sb = sb("pk_sb", [P, 2 * KV], BF16)
    e_q = sb("e_q", [P, 4 * 512], F32)
    skrow = sb("skrow", [1, 1024], BF16)
    skb_bc = sb("skb_bc", [P, KV], BF16)
    svb_bc = sb("svb_bc", [P, KV], BF16)
    sq_cs = sb("sq_cs", [P, FC], F32)
    stg_r = sb("stg_r", [P, 8], F32)
    stg_nm = sb("stg_nm", [P, 8], F32)
    rows_r = sb("rows_r", [8, P], BF16)
    rows_nm = sb("rows_nm", [8, P], BF16)
    rq_bc = sb("rq_bc", [P, 1024], BF16)
    nm_bc = sb("nm_bc", [P, 1024], BF16)
    ps_tr = pt("ps_tr", [P, 2048], BF16)
    ps_big = pt("ps_big", [P, 2048], F32)
    ps_ka0 = pt("ps_ka0", [P, 512], F32)
    ps_ka1 = pt("ps_ka1", [P, 512], F32)
    E["_p1_stack"] = p1

    xn1v = xn1_sb[:].rearrange("p (f t) -> p f t", f=FC)
    wkvv = wkv_sb[:].rearrange("p (k m) -> p k m", k=FC)
    wqv = wq_sb[:].rearrange("p (b k m) -> p b k m", b=2, k=FC)
    ps_ka = [ps_ka0, ps_ka1]

    @block.sync
    def _(sync):
        for gt in range(TC):
            if gt >= 2:
                sync.wait_ge(c_sq.h, gt - 1)
                sync.wait_ge(c_red.h, gt - 1)
            s_x[gt % 2].inc(sync.dma_start(
                out=x_sb[:, (gt % 2) * HID:(gt % 2 + 1) * HID],
                in_=x_in[gt * P:(gt + 1) * P, :]), 16)
        sync.wait_ge(c_ccp.h, 1)
        s_cc.inc(sync.dma_start(out=cc_in_t.ap(), in_=cc_sb[:]), 16)
        sync.wait_ge(s_cc.h, 17)
        s_cc.inc(sync.dma_start(out=kvf_sb[:], in_=cc_out_t.ap()), 16)

    @block.gpsimd
    def _(gp):
        s_wkv.inc(gp.dma_start(out=wkv_sb[:], in_=wkv_blk), 16)
        for ci in range(8):
            if ci >= 2:
                gp.wait_ge(c_qpe.h, 4 * (ci - 1))
            s_wq[ci % 2].inc(gp.dma_start(
                out=wq_sb[:, (ci % 2) * FC * 256:(ci % 2 + 1) * FC * 256],
                in_=wq_blk[ci % 8, :, :]), 16)
        gp.wait_ge(s_cc.h, 16)
        if DEBUG_NO_CC:
            s_cc.inc(gp.dma_start(out=cc_out_t.ap(), in_=cc_in_t.ap()), 16)
        else:
            s_cc.inc(gp.collective_compute(
                "AllReduce", ALU.add,
                replica_groups=[[0, 1, 2, 3], [4, 5, 6, 7]],
                ins=[cc_in_t.ap().opt()],
                outs=[cc_out_t.ap().opt()]), 1)
        for ci in range(8, 16):
            gp.wait_ge(c_qpe.h, 4 * (ci - 1))
            s_wq[ci % 2].inc(gp.dma_start(
                out=wq_sb[:, (ci % 2) * FC * 256:(ci % 2 + 1) * FC * 256],
                in_=wq_blk[ci % 8, :, :]), 16)

    @block.tensor
    def _(t):
        t.wait_ge(c_bias.h, 4)
        t.wait_ge(s_wkv.h, 16)
        ins = None
        for nn in range(2):
            for k in range(FC):
                ins = t.matmul(ps_big[0:1, nn * 512:(nn + 1) * 512],
                               ones_col_bf[:], wkvv[:, k, nn * 512:(nn + 1) * 512],
                               start=(k == 0), stop=(k == FC - 1))
        c_bias.inc(ins)
        t.wait_ge(c_bias.h, 6)
        t.matmul(ps_big[:, 0:512], ones_row_bf[:],
                 skrow[:, 0:512], start=True, stop=True)
        c_bias.inc(t.matmul(ps_big[:, 512:1024], ones_row_bf[:],
                            skrow[:, 512:1024], start=True, stop=True))

        ps_bigbf = ps_big[:].bitcast(BF16)

        def transp(gt):
            par = gt % 2
            t.wait_ge(c_ln.h, gt + 1)
            if gt == 0:
                t.wait_ge(c_bias.h, 8)
            else:
                t.wait_ge(c_xn1.h, 4 * gt)
                t.wait_ge(c_skv.h, gt - 1)
            if gt == TCH:
                t.wait_ge(c_qbc.h, 2)
                t.wait_ge(c_qev.h, 32)
            for gr in range(4):
                G = gt * 4 + gr
                ins = None
                for j in range(4):
                    fc = gr * 4 + j
                    if gr < 2:
                        dst = ps_tr[:, gr * 1024 + j * P:gr * 1024 + (j + 1) * P]
                    else:
                        dst = ps_bigbf[:, par * 2048 + (gr - 2) * 1024 + j * P:
                                       par * 2048 + (gr - 2) * 1024 + (j + 1) * P]
                    ins = t.transpose(
                        dst,
                        xnt_sb[:, par * HID + fc * P:par * HID + (fc + 1) * P],
                        ident_bf[:])
                c_tr.inc(ins)

        transp(0)
        for half in range(HALF):
            for i in range(TCH):
                gt = half * TCH + i
                par = gt % 2
                if half == 1 and i == 0:
                    transp(gt)
                if gt + 1 < TC and (gt + 1) % TCH != 0:
                    transp(gt + 1)
                t.wait_ge(c_xn1.h, 4 * (gt + 1))
                if gt == 0:
                    t.wait_ge(c_bias.h, 8)
                if gt >= 2:
                    t.wait_ge(c_skv.h, gt - 1)
                if half == 1 and i == 0:
                    t.wait_ge(c_qev.h, 32)
                ins = None
                for nn in range(2):
                    for k in range(FC):
                        ins = t.matmul(
                            ps_big[:, par * 1024 + nn * 512:par * 1024 + (nn + 1) * 512],
                            xn1v[:, k, i * P:(i + 1) * P],
                            wkvv[:, k, nn * 512:(nn + 1) * 512],
                            start=(k == 0), stop=(k == FC - 1))
                c_kvp.inc(ins)
                for kprev in ([gt - 1] if gt >= 1 else []) + ([gt] if gt == TC - 1 else []):
                    kpar = kprev % 2
                    t.wait_ge(c_skv.h, kprev + 1)
                    ins = None
                    for kvh in range(KVH):
                        ins = t.matmul(
                            ps_ka[kvh // 2][:, (kvh % 2) * 130:(kvh % 2) * 130 + 129],
                            sk_sb[:, kpar * KV + kvh * P:kpar * KV + (kvh + 1) * P],
                            v_sb[:, kpar * KVH * 130 + kvh * 130:kpar * KVH * 130 + kvh * 130 + 129],
                            start=False, stop=(kprev == TC - 1), skip_group_check=True)
                    c_ka.inc(ins)
            t.wait_ge(c_skv.h, (half + 1) * TCH - 1)
            t.wait_ge(c_stg.h, (half + 1) * TCH)
            t.transpose(ps_big[0:8, 0:P], stg_r[:, 0:8], ident_f[:])
            c_rowp.inc(t.transpose(ps_big[0:8, 512:512 + P],
                                   stg_nm[:, 0:8], ident_f[:]))
            t.wait_ge(c_row.h, half + 1)
            t.wait_ge(c_skv.h, (half + 1) * TCH)
            ps_trf = ps_tr[:].bitcast(F32)
            ins = None
            for j in range(8):
                ins = t.matmul(ps_big[:, 1024 + j * P:1024 + (j + 1) * P],
                               sel8_sb[:, j * P:(j + 1) * P], rows_r[:],
                               start=True, stop=True, skip_group_check=True)
                ins = t.matmul(ps_trf[:, j * P:(j + 1) * P],
                               sel8_sb[:, j * P:(j + 1) * P], rows_nm[:],
                               start=True, stop=True, skip_group_check=True)
            c_qb3.inc(ins)
            for qg in range(8):
                ci = half * 8 + qg
                t.wait_ge(s_wq[ci % 2].h, 16 * (ci // 2 + 1))
                if half == 0:
                    t.wait_ge(c_sqs.h, ci)
                    ins = None
                    for hh in range(2):
                        for k in range(FC):
                            ins = t.matmul(ps_ka1[:, 384 + hh:385 + hh],
                                           wqv[:, ci % 2, k, hh * P:(hh + 1) * P],
                                           ones_col_bf[:],
                                           start=False, stop=True, skip_group_check=True)
                    c_sqp.inc(ins)
                for hh in range(2):
                    for n in range(2):
                        idx = ci * 4 + hh * 2 + n
                        if idx % 32 == 2:
                            t.wait_ge(c_qbc.h, 2 * half + 1)
                        if idx >= 4:
                            t.wait_ge(c_qev.h, idx - 3)
                        ins = None
                        for k in range(FC):
                            ins = t.matmul(
                                ps_big[:, (idx % 4) * 512:(idx % 4 + 1) * 512],
                                wqv[:, ci % 2, k, hh * P:(hh + 1) * P],
                                xn1v[:, k, n * 512:(n + 1) * 512],
                                start=(k == 0), stop=(k == FC - 1))
                        c_qpe.inc(ins)

    @block.scalar
    def _(s):
        def cast_act(gt):
            par = gt % 2
            xt = x_sb[:, par * HID:(par + 1) * HID]
            s.wait_ge(s_x[gt % 2].h, 16 * (gt // 2 + 1))
            if gt >= 2:
                s.wait_ge(c_tr.h, 4 * gt - 4)
            c_ln.inc(s.activation(out=xnt_sb[:, par * HID:(par + 1) * HID], in_=xt,
                                  func=AF.Identity))

        def sq_act(gt):
            par = gt % 2
            sc = par * 12
            xt = x_sb[:, par * HID:(par + 1) * HID]
            s.wait_ge(c_red.h, gt + 1)
            s.wait_ge(c_ln.h, gt + 1)
            c_sq.inc(s.activation(out=xt, in_=xt, func=AF.Square,
                                  accum_out=stat[:, sc + 1:sc + 2]))

        def lnexp_act(gt):
            sc = (gt % 2) * 12
            s.wait_ge(c_var.h, gt + 1)
            c_ac.inc(s.activation(out=stat[:, sc + 6:sc + 7], in_=stat[:, sc + 5:sc + 6],
                                  func=AF.Ln, bias=eps_ln[:]))
            c_ac.barrier(s)
            c_rstd.inc(s.activation(out=stat[:, sc + 7:sc + 8], in_=stat[:, sc + 6:sc + 7],
                                    func=AF.Exp, scale=-0.5))
            c_rstd.barrier(s)

        cast_act(0)
        sq_act(0)
        cast_act(1)
        sq_act(1)
        lnexp_act(0)
        for half in range(HALF):
            for i in range(TCH):
                gt = half * TCH + i
                par = gt % 2
                if gt + 2 < TC:
                    cast_act(gt + 2)
                    sq_act(gt + 2)
                if gt + 1 < TC:
                    lnexp_act(gt + 1)
                s.wait_ge(c_pk.h, gt + 1)
                pk = pk_sb[:, par * KV:(par + 1) * KV]
                ek = e_q[:, par * 512:(par + 1) * 512]
                c_ac.inc(s.activation(out=ek, in_=pk, func=AF.Relu, scale=-1.0, bias=1.0))
                c_ac.barrier(s)
                c_e.inc(s.activation(out=ek, in_=ek, func=AF.Exp, scale=-1.0))
            for qg in range(8):
                ci = half * 8 + qg
                for hh in range(2):
                    qc = qg * 2 + hh
                    for n in range(2):
                        idx = ci * 4 + hh * 2 + n
                        s.wait_ge(c_qln.h, idx + 1)
                        if idx >= 2:
                            s.wait_ge(c_qev.h, idx - 1)
                        t1s = e_q[:, (idx % 2) * 512:(idx % 2 + 1) * 512]
                        ers = e_q[:, 1024 + (idx % 2) * 512:1024 + (idx % 2 + 1) * 512]
                        c_ac.inc(s.activation(out=ers, in_=t1s, func=AF.Relu, scale=-1.0,
                                              bias=negbq_sb[:, qc:qc + 1]))
                        c_ac.barrier(s)
                        c_qact.inc(s.activation(out=ers, in_=ers, func=AF.Exp, scale=-1.0))

    @block.vector
    def _(v):
        v.memset(ps_ka0[:], 0.0)
        v.memset(ps_ka1[:], 0.0)
        v.wait_ge(c_bias.h, 2)
        v.memset(cc_sb[:], 0.0)
        c_bias.inc(v.memset(
            v_sb[:].rearrange("p (g c) -> p g c", c=130)[:, :, 128:130], 1.0))
        v.wait_ge(c_bias.h, 5)
        c_bias.inc(v.tensor_copy(out=skrow[:], in_=ps_big[0:1, 0:1024]))
        v.wait_ge(c_bias.h, 7)
        v.tensor_copy(out=skb_bc[:], in_=ps_big[:, 0:512])
        c_bias.inc(v.tensor_copy(out=svb_bc[:], in_=ps_big[:, 512:1024]))

        def ln_dve(gt):
            par = gt % 2
            sc = par * 12
            xt = x_sb[:, par * HID:(par + 1) * HID]
            v.wait_ge(s_x[gt % 2].h, 16 * (gt // 2 + 1))
            c_red.inc(v.reduce_sum(out=stat[:, sc + 0:sc + 1], in_=xt, axis=AX.X))
            c_red.barrier(v)

        def chain_dve(gt):
            par = gt % 2
            sc = par * 12
            li = gt % TCH
            v.wait_ge(c_sq.h, gt + 1)
            c_dv.inc(v.tensor_scalar(out=stat[:, sc + 2:sc + 3], in0=stat[:, sc + 0:sc + 1],
                                     scalar1=INV_N, scalar2=None, op0=ALU.mult))
            c_dv.barrier(v)
            c_dv.inc(v.tensor_tensor(out=stat[:, sc + 4:sc + 5], in0=stat[:, sc + 2:sc + 3],
                                     in1=stat[:, sc + 2:sc + 3], op=ALU.mult))
            c_dv.barrier(v)
            c_var.inc(v.tensor_scalar(out=stat[:, sc + 5:sc + 6], in0=stat[:, sc + 1:sc + 2],
                                      scalar1=INV_N, scalar2=stat[:, sc + 4:sc + 5],
                                      op0=ALU.mult, op1=ALU.subtract))
            v.wait_ge(c_rstd.h, gt + 1)
            c_negmr.inc(v.tensor_scalar(out=stat[:, sc + 8:sc + 9], in0=stat[:, sc + 2:sc + 3],
                                        scalar1=stat[:, sc + 7:sc + 8], scalar2=-1.0,
                                        op0=ALU.mult, op1=ALU.mult))
            c_negmr.barrier(v)
            if gt == TCH:
                v.wait_ge(c_rowp.h, 1)
            v.tensor_copy(out=stg_r[:, li:li + 1], in_=stat[:, sc + 7:sc + 8])
            c_stg.inc(v.tensor_copy(out=stg_nm[:, li:li + 1], in_=stat[:, sc + 8:sc + 9]))

        ps_bigbfv = ps_big[:].bitcast(BF16)

        def xn1_evac(gt):
            li = gt % TCH
            par = gt % 2
            if gt == TCH:
                v.wait_ge(c_qpe.h, 32)
            for gr in range(4):
                G = gt * 4 + gr
                v.wait_ge(c_tr.h, G + 1)
                if gr < 2:
                    src = ps_tr[:, gr * 1024:gr * 1024 + 512]
                else:
                    src = ps_bigbfv[:, par * 2048 + (gr - 2) * 1024:
                                    par * 2048 + (gr - 2) * 1024 + 512]
                c_xn1.inc(v.tensor_copy(
                    out=xn1v[:, gr * 4:(gr + 1) * 4, li * P:(li + 1) * P],
                    in_=src.rearrange("p (a b) -> p a b", a=4)))

        ln_dve(0)
        ln_dve(1)
        chain_dve(0)
        xn1_evac(0)
        for half in range(HALF):
            for i in range(TCH):
                gt = half * TCH + i
                par = gt % 2
                sc = par * 12
                if half == 1 and i == 0:
                    xn1_evac(gt)
                    if gt + 1 < TC:
                        xn1_evac(gt + 1)
                    if gt + 2 < TC:
                        ln_dve(gt + 2)
                    if gt + 1 < TC:
                        chain_dve(gt + 1)
                else:
                    if gt + 1 < TC and (gt + 1) % TCH != 0:
                        xn1_evac(gt + 1)
                    if gt + 2 < TC:
                        ln_dve(gt + 2)
                    if gt + 1 < TC:
                        chain_dve(gt + 1)
                v.wait_ge(c_kvp.h, gt + 1)
                if gt >= 2:
                    v.wait_ge(c_e.h, gt - 1)
                    v.wait_ge(c_ka.h, gt - 1)
                kreg = ps_big[:, par * 1024:par * 1024 + 512]
                vreg = ps_big[:, par * 1024 + 512:par * 1024 + 1024]
                basek = e_q[:, 1024:1536]
                basev = e_q[:, 1536:2048]
                v.wait_ge(c_skv.h, gt)
                v.scalar_tensor_tensor(out=basek, in0=skb_bc[:], scalar=stat[:, sc + 8:sc + 9],
                                       in1=bk1_bc[:], op0=ALU.mult, op1=ALU.add)
                c_dv.inc(v.scalar_tensor_tensor(out=basev, in0=svb_bc[:],
                                                scalar=stat[:, sc + 8:sc + 9],
                                                in1=bv_bc[:], op0=ALU.mult, op1=ALU.add))
                c_dv.barrier(v)
                c_pk.inc(v.scalar_tensor_tensor(out=pk_sb[:, par * KV:(par + 1) * KV],
                                                in0=kreg, scalar=stat[:, sc + 7:sc + 8],
                                                in1=basek, op0=ALU.mult, op1=ALU.add))
                v.wait_ge(c_e.h, gt + 1)
                v.tensor_tensor(out=sk_sb[:, par * KV:(par + 1) * KV],
                                in0=pk_sb[:, par * KV:(par + 1) * KV],
                                in1=e_q[:, par * 512:(par + 1) * 512], op=ALU.max)
                c_skv.inc(v.scalar_tensor_tensor(
                    out=v_sb[:, par * KVH * 130:(par + 1) * KVH * 130]
                    .rearrange("p (g c) -> p g c", g=KVH)[:, :, 0:128],
                    in0=vreg, scalar=stat[:, sc + 7:sc + 8],
                    in1=basev, op0=ALU.mult, op1=ALU.add))
            if half == 1:
                v.wait_ge(c_ka.h, TC)
                ins = None
                for kvh in range(KVH):
                    ins = v.tensor_copy(
                        out=cc_sb[:, kvh * 130:kvh * 130 + 129],
                        in_=ps_ka[kvh // 2][:, (kvh % 2) * 130:(kvh % 2) * 130 + 129])
                c_ccp.inc(ins)
            v.wait_ge(c_rowp.h, half + 1)
            v.tensor_copy(out=rows_r[:], in_=ps_big[0:8, 0:P])
            c_row.inc(v.tensor_copy(out=rows_nm[:], in_=ps_big[0:8, 512:512 + P]))
            v.wait_ge(c_qb3.h, half + 1)
            ps_trfv = ps_tr[:].bitcast(F32)
            v.tensor_copy(out=rq_bc[:, 0:512], in_=ps_big[:, 1024:1536])
            c_qbc.inc(v.tensor_copy(out=rq_bc[:, 512:1024], in_=ps_big[:, 1536:2048]))
            v.tensor_copy(out=nm_bc[:, 0:512], in_=ps_trfv[:, 0:512])
            c_qbc.inc(v.tensor_copy(out=nm_bc[:, 512:1024], in_=ps_trfv[:, 512:1024]))
            c_qbc.barrier(v)
            for qg in range(8):
                ci = half * 8 + qg
                qc0 = qg * 2
                if half == 0:
                    v.wait_ge(c_sqp.h, ci + 1)
                    c_dv.inc(v.tensor_copy(out=sq_cs[:, qc0:qc0 + 2], in_=ps_ka1[:, 384:386]))
                    c_dv.barrier(v)
                    c_sqs.inc(v.memset(ps_ka1[:, 384:386], 0.0))
                def q_max(idx):
                    qc_ = (idx % 32) // 4 * 2 + (idx % 4) // 2
                    qcq = ((idx // 4) % 8) * 2 + (idx % 4) // 2
                    n_ = idx % 2
                    hf = idx // 32
                    t1s = e_q[:, (idx % 2) * 512:(idx % 2 + 1) * 512]
                    ers = e_q[:, 1024 + (idx % 2) * 512:1024 + (idx % 2 + 1) * 512]
                    v.wait_ge(c_qact.h, idx + 1)
                    col = qcq * T + hf * TTH + n_ * 512
                    c_qev.inc(v.scalar_tensor_tensor(
                        out=sq_sb[:, col:col + 512], in0=t1s,
                        scalar=bq1_sb[:, qcq:qcq + 1], in1=ers,
                        op0=ALU.add, op1=ALU.max))

                for hh in range(2):
                    qc = qg * 2 + hh
                    for n in range(2):
                        idx = ci * 4 + hh * 2 + n
                        if idx % 32 != 0:
                            q_max(idx - 1)
                        v.wait_ge(c_qpe.h, idx + 1)
                        if idx >= 2:
                            v.wait_ge(c_qev.h, idx - 1)
                        t1s = e_q[:, (idx % 2) * 512:(idx % 2 + 1) * 512]
                        reg = ps_big[:, (idx % 4) * 512:(idx % 4 + 1) * 512]
                        c_dv.inc(v.tensor_tensor(out=t1s, in0=reg,
                                 in1=rq_bc[:, n * 512:(n + 1) * 512], op=ALU.mult))
                        c_dv.barrier(v)
                        c_qln.inc(v.scalar_tensor_tensor(
                            out=t1s, in0=nm_bc[:, n * 512:(n + 1) * 512],
                            scalar=sq_cs[:, qc:qc + 1], in1=t1s,
                            op0=ALU.mult, op1=ALU.add))
            if True:
                q_max(half * 32 + 31)
        v.wait_ge(s_cc.h, 33)
        for kvh in range(KVH):
            v.tensor_copy(out=kv_bf[:, kvh * P:(kvh + 1) * P],
                          in_=kvf_sb[:, kvh * 130:kvh * 130 + 128])
        c_kvbf.inc(v.tensor_copy(
            out=ksum_bf[:],
            in_=kvf_sb[:].rearrange("p (g c) -> p g c", g=KVH)[:, :, 128:129]
            .rearrange("p g c -> p (g c)")))

    p1.close()


def _p3(nc, block, E):
    x_in = E["x_in"]; sq_sb = E["sq_sb"]; stat = E["stat"]
    ones_row_f = E["ones_row_f"]; kv_bf = E["kv_bf"]; ksum_bf = E["ksum_bf"]
    wo_blk = E["wo_blk"]; x1_sp = E["x1_sp"]; xn2_sp = E["xn2_sp"]
    debug = E["debug"]; dbg = E["dbg"]
    s_wo = E["s_wo"]; s_x3 = E["s_x3"]; s_spx = E["s_spx"]; s_spn = E["s_spn"]
    s_cc = E["s_cc"]
    c_qpe = E["c_qpe"]; c_qev = E["c_qev"]; c_kvbf = E["c_kvbf"]; c_qact = E["c_qact"]
    c_n3a = E["c_n3a"]; c_nact = E["c_nact"]; c_b3a = E["c_b3a"]; c_inv = E["c_inv"]
    c_q3a = E["c_q3a"]; c_attn = E["c_attn"]; c_ope = E["c_ope"]; c_x1e = E["c_x1e"]
    c2_red = E["c2_red"]; c2_sq = E["c2_sq"]; c2_var = E["c2_var"]; c2_rstd = E["c2_rstd"]
    c2_negmr = E["c2_negmr"]; c_ln2 = E["c_ln2"]
    c_dv = E["c_dv"]; c_ac = E["c_ac"]
    eps_ln = E["eps_ln"]; eps_at = E["eps_at"]

    p3 = contextlib.ExitStack()
    sb = lambda *a: p3.enter_context(nc.sbuf_tensor(*a))
    pt = lambda *a: p3.enter_context(nc.psum_tensor(*a))
    wo_sb = sb("wo_sb", [P, 4 * FC * 512], BF16)
    x_sb3 = sb("x_sb3", [P, 2 * HID], F32)
    x1_sb = sb("x1_sb", [P, 2 * HID], F32)
    xn2t_sb = sb("xn2t_sb", [P, 2 * HID], BF16)
    sq_scr3 = sb("sq_scr3", [P, HID], F32)
    normb_sb = sb("normb_sb", [1, 1024], F32)
    invb_sb = sb("invb_sb", [P, 1024], F32)
    ps_n = pt("ps_n", [1, 1024], F32)
    ps_b = pt("ps_b", [P, 1024], F32)
    ps_qk = pt("ps_qk", [P, 1024], F32)
    ps_o = pt("ps_o", [P, 1024], F32)

    wov = wo_sb[:].rearrange("p (o k m) -> p o k m", o=4, k=FC)

    s_wol = E["s_wol"]

    @block.gpsimd
    def _(gp):
        gp.wait_ge(c_qev.h, 64)
        gp.wait_ge(c_qpe.h, 64)
        gp.wait_ge(s_cc.h, 33)
        for og in range(4):
            s_wol[og].inc(gp.dma_start(out=wo_sb[:, og * FC * 512:(og + 1) * FC * 512],
                                       in_=wo_blk[og, :, :]), 16)

    @block.tensor
    def _(t):
        t.wait_ge(c_kvbf.h, 1)
        for step in range(65):
            if step < 64:
                idx = step
                h, n = idx // 4, idx % 4
                if idx >= 2:
                    t.wait_ge(c_nact.h, idx - 1)
                c_n3a.inc(t.matmul(
                    ps_n[:, (idx % 2) * 512:(idx % 2 + 1) * 512],
                    ksum_bf[:, (h // 4):(h // 4) + 1],
                    sq_sb[:, h * T + n * 512:h * T + (n + 1) * 512],
                    start=True, stop=True))
            if step >= 1:
                idx = step - 1
                h, n = idx // 4, idx % 4
                t.wait_ge(c_nact.h, idx + 1)
                if idx >= 2:
                    t.wait_ge(c_inv.h, idx - 1)
                c_b3a.inc(t.matmul(
                    ps_b[:, (idx % 2) * 512:(idx % 2 + 1) * 512],
                    ones_row_f[:],
                    normb_sb[:, (idx % 2) * 512:(idx % 2 + 1) * 512],
                    start=True, stop=True))
                if idx >= 2:
                    t.wait_ge(c_attn.h, idx - 1)
                c_q3a.inc(t.matmul(
                    ps_qk[:, (idx % 2) * 512:(idx % 2 + 1) * 512],
                    kv_bf[:, (h // 4) * P:(h // 4 + 1) * P],
                    sq_sb[:, h * T + n * 512:h * T + (n + 1) * 512],
                    start=True, stop=True))
        for tc in range(TC):
            for og in range(4):
                oidx = tc * 4 + og
                if oidx >= 2:
                    t.wait_ge(c_x1e.h, oidx - 1)
                if oidx == 0:
                    t.wait_ge(c_attn.h, 64)
                if tc == 0:
                    t.wait_ge(s_wol[og].h, 16)
                ins = None
                for k in range(FC):
                    ins = t.matmul(
                        ps_o[:, (oidx % 2) * 512:(oidx % 2 + 1) * 512],
                        sq_sb[:, k * T + tc * P:k * T + tc * P + P],
                        wov[:, og, k, :],
                        start=(k == 0), stop=(k == FC - 1))
                c_ope.inc(ins)

    @block.sync
    def _(sync):
        def spill(tp):
            sync.wait_ge(c_ln2.h, tp + 1)
            s_spx[tp % 2].inc(sync.dma_start(out=x1_sp[tp * P:(tp + 1) * P, :],
                                             in_=x1_sb[:, (tp % 2) * HID:(tp % 2 + 1) * HID]), 16)
            s_spn[tp % 2].inc(sync.dma_start(out=xn2_sp[tp * P:(tp + 1) * P, :],
                                             in_=xn2t_sb[:, (tp % 2) * HID:(tp % 2 + 1) * HID]), 16)
        for tc in range(TC):
            if tc == 0:
                sync.wait_ge(c_qpe.h, 64)
                sync.wait_ge(c_kvbf.h, 1)
                sync.wait_ge(c_qact.h, 64)
            if tc >= 2:
                sync.wait_ge(c_x1e.h, 4 * tc - 4)
            s_x3[tc % 2].inc(sync.dma_start(
                out=x_sb3[:, (tc % 2) * HID:(tc % 2 + 1) * HID],
                in_=x_in[tc * P:(tc + 1) * P, :]), 16)
            if tc >= 2:
                spill(tc - 2)
        spill(TC - 2)
        spill(TC - 1)

    @block.scalar
    def _(s):
        for idx in range(64):
            s.wait_ge(c_n3a.h, idx + 1)
            if idx >= 2:
                s.wait_ge(c_b3a.h, idx - 1)
            c_nact.inc(s.activation(
                out=normb_sb[:, (idx % 2) * 512:(idx % 2 + 1) * 512],
                in_=ps_n[:, (idx % 2) * 512:(idx % 2 + 1) * 512],
                func=AF.Identity, bias=eps_at[0:1, :]))
        for tc in range(TC):
            par = tc % 2
            sc = par * 12
            x1t = x1_sb[:, par * HID:(par + 1) * HID]
            s.wait_ge(c2_red.h, tc + 1)
            c2_sq.inc(s.activation(out=sq_scr3[:], in_=x1t, func=AF.Square,
                                   accum_out=stat[:, sc + 1:sc + 2]))
            s.wait_ge(c2_var.h, tc + 1)
            c_ac.inc(s.activation(out=stat[:, sc + 6:sc + 7], in_=stat[:, sc + 5:sc + 6],
                                  func=AF.Ln, bias=eps_ln[:]))
            c_ac.barrier(s)
            c2_rstd.inc(s.activation(out=stat[:, sc + 7:sc + 8], in_=stat[:, sc + 6:sc + 7],
                                     func=AF.Exp, scale=-0.5))
            c2_rstd.barrier(s)
            s.wait_ge(c2_negmr.h, tc + 1)
            if tc >= 2:
                s.wait_ge(s_spn[tc % 2].h, 16 * ((tc - 2) // 2 + 1))
            c_ln2.inc(s.activation(out=xn2t_sb[:, par * HID:(par + 1) * HID], in_=x1t,
                                   func=AF.Identity, scale=stat[:, sc + 7:sc + 8],
                                   bias=stat[:, sc + 8:sc + 9]))

    @block.vector
    def _(v):
        for idx in range(64):
            h, n = idx // 4, idx % 4
            v.wait_ge(c_b3a.h, idx + 1)
            if idx >= 2:
                v.wait_ge(c_attn.h, idx - 1)
            c_inv.inc(v.reciprocal(out=invb_sb[:, (idx % 2) * 512:(idx % 2 + 1) * 512],
                                   in_=ps_b[:, (idx % 2) * 512:(idx % 2 + 1) * 512]))
            c_inv.barrier(v)
            v.wait_ge(c_q3a.h, idx + 1)
            c_attn.inc(v.tensor_tensor(
                out=sq_sb[:, h * T + n * 512:h * T + (n + 1) * 512],
                in0=ps_qk[:, (idx % 2) * 512:(idx % 2 + 1) * 512],
                in1=invb_sb[:, (idx % 2) * 512:(idx % 2 + 1) * 512], op=ALU.mult))
        for tc in range(TC):
            par = tc % 2
            sc = par * 12
            for og in range(4):
                oidx = tc * 4 + og
                v.wait_ge(c_ope.h, oidx + 1)
                v.wait_ge(s_x3[tc % 2].h, 16 * (tc // 2 + 1))
                if og == 0 and tc >= 2:
                    v.wait_ge(s_spx[tc % 2].h, 16 * ((tc - 2) // 2 + 1))
                c_x1e.inc(v.tensor_tensor(
                    out=x1_sb[:, par * HID + og * 512:par * HID + (og + 1) * 512],
                    in0=ps_o[:, (oidx % 2) * 512:(oidx % 2 + 1) * 512],
                    in1=x_sb3[:, par * HID + og * 512:par * HID + (og + 1) * 512],
                    op=ALU.add))
            x1t = x1_sb[:, par * HID:(par + 1) * HID]
            c_x1e.barrier(v)
            c2_red.inc(v.reduce_sum(out=stat[:, sc + 0:sc + 1], in_=x1t, axis=AX.X))
            c2_red.barrier(v)
            v.wait_ge(c2_sq.h, tc + 1)
            c_dv.inc(v.tensor_scalar(out=stat[:, sc + 2:sc + 3], in0=stat[:, sc + 0:sc + 1],
                                     scalar1=INV_N, scalar2=None, op0=ALU.mult))
            c_dv.barrier(v)
            c_dv.inc(v.tensor_tensor(out=stat[:, sc + 4:sc + 5], in0=stat[:, sc + 2:sc + 3],
                                     in1=stat[:, sc + 2:sc + 3], op=ALU.mult))
            c_dv.barrier(v)
            c2_var.inc(v.tensor_scalar(out=stat[:, sc + 5:sc + 6], in0=stat[:, sc + 1:sc + 2],
                                       scalar1=INV_N, scalar2=stat[:, sc + 4:sc + 5],
                                       op0=ALU.mult, op1=ALU.subtract))
            v.wait_ge(c2_rstd.h, tc + 1)
            c2_negmr.inc(v.tensor_scalar(out=stat[:, sc + 8:sc + 9], in0=stat[:, sc + 2:sc + 3],
                                         scalar1=stat[:, sc + 7:sc + 8], scalar2=-1.0,
                                         op0=ALU.mult, op1=ALU.mult))

    p3.close()


def _p4(nc, block, E):
    x1_sp = E["x1_sp"]; xn2_sp = E["xn2_sp"]; out = E["out"]
    wgu_blk = E["wgu_blk"]; wd_blk = E["wd_blk"]
    ident_bf = E["ident_bf"]; ident_f = E["ident_f"]
    s_wgu = E["s_wgu"]; s_wd = E["s_wd"]; s_x2 = E["s_x2"]; s_x1r = E["s_x1r"]
    s_out = E["s_out"]; s_spx = E["s_spx"]; s_spn = E["s_spn"]
    c_ope = E["c_ope"]; c_x1e = E["c_x1e"]; c_attn = E["c_attn"]; c_ln2 = E["c_ln2"]
    c_x2t = E["c_x2t"]; c_x2f = E["c_x2f"]
    c_gpe = E["c_gpe"]; c_upe = E["c_upe"]; c_sil = E["c_sil"]; c_guev = E["c_guev"]
    c_dpe = E["c_dpe"]; c_dev = E["c_dev"]; c_dwd = E["c_dwd"]; c_ddone = E["c_ddone"]
    c_ft = E["c_ft"]; c_fev = E["c_fev"]; c_fin = E["c_fin"]
    debug = E["debug"]

    p4 = contextlib.ExitStack()
    sb = lambda *a: p4.enter_context(nc.sbuf_tensor(*a))
    pt = lambda *a: p4.enter_context(nc.psum_tensor(*a))
    acc_sb = sb("acc_sb", [P, FC * TTG], F32)
    gated_sb = sb("gated_sb", [P, 16 * TTG], BF16)
    xn2f_sb = sb("xn2f_sb", [P, FC * TTG], BF16)
    wgu_sb = sb("wgu_sb", [P, 2 * FC * 256], BF16)
    wd_sb = sb("wd_sb", [P, 2 * FC * 256], BF16)
    x1_sb4 = sb("x1_sb4", [P, 2 * HID], F32)
    sg_tmp = sb("sg_tmp", [P, 1024], F32)
    ps_gu = pt("ps_gu", [P, 2048], F32)
    ps_tf = pt("ps_tf", [P, 1024], F32)

    accv = acc_sb[:].rearrange("p (f t) -> p f t", f=FC)
    gatv = gated_sb[:].rearrange("p (k t) -> p k t", k=16)
    xn2fv = xn2f_sb[:].rearrange("p (f t) -> p f t", f=FC)
    wguv = wgu_sb[:].rearrange("p (b k m) -> p b k m", b=2, k=FC)
    wdv = wd_sb[:].rearrange("p (b k m) -> p b k m", b=2, k=FC)

    @block.gpsimd
    def _(gp):
        gp.wait_ge(c_ope.h, 64)
        gp.wait_ge(c_x1e.h, 64)
        gp.wait_ge(c_ln2.h, 16)
        for tg in range(NTG):
            for r in range(NR):
                for hl in range(16):
                    hcb = r * 16 + hl
                    ci = tg * 64 + hcb
                    if ci >= 2:
                        gp.wait_ge(c_upe.h, 2 * (ci - 1))
                    s_wgu[ci % 2].inc(gp.dma_start(
                        out=wgu_sb[:, (ci % 2) * FC * 256:(ci % 2 + 1) * FC * 256],
                        in_=wgu_blk[hcb, :, :]), 16)
                for fg in range(FG):
                    wj = tg * 32 + r * 8 + fg
                    if wj >= 2:
                        gp.wait_ge(c_dpe.h, 4 * (wj - 1))
                    s_wd[wj % 2].inc(gp.dma_start(
                        out=wd_sb[:, (wj % 2) * FC * 256:(wj % 2 + 1) * FC * 256],
                        in_=wd_blk[r * 8 + fg, :, :]), 16)

    @block.sync
    def _(sync):
        for tg in range(NTG):
            pass
            for i in range(8):
                gi = tg * 8 + i
                if gi == 0:
                    for s_ in (s_spx[0], s_spx[1], s_spn[0], s_spn[1]):
                        sync.wait_ge(s_.h, s_.n)
                    sync.wait_ge(c_ope.h, 64)
                    sync.wait_ge(c_ln2.h, 16)
                if i >= 2:
                    gp_ = gi - 2
                    sync.wait_ge(c_fev.h, 4 * (gp_ + 1))
                    s_out[gp_ % 2].inc(sync.dma_start(
                        out=out[gp_ * P:(gp_ + 1) * P, :],
                        in_=x1_sb4[:, (gp_ % 2) * HID:(gp_ % 2 + 1) * HID]), 16)
                    pass
                if gi >= 2:
                    sync.wait_ge(s_out[gi % 2].h, 16 * ((gi - 2) // 2 + 1))
                s_x1r[gi % 2].inc(sync.dma_start(
                    out=x1_sb4[:, (gi % 2) * HID:(gi % 2 + 1) * HID],
                    in_=x1_sp[gi * P:(gi + 1) * P, :]), 16)
            for gp_ in (tg * 8 + 6, tg * 8 + 7):
                sync.wait_ge(c_fev.h, 4 * (gp_ + 1))
                s_out[gp_ % 2].inc(sync.dma_start(
                    out=out[gp_ * P:(gp_ + 1) * P, :],
                    in_=x1_sb4[:, (gp_ % 2) * HID:(gp_ % 2 + 1) * HID]), 16)

    @block.tensor
    def _(t):
        for tg in range(NTG):
            for r in range(NR):
                for hl in range(16):
                    hcb = r * 16 + hl
                    ci = tg * 64 + hcb
                    t.wait_ge(s_wgu[ci % 2].h, 16 * (ci // 2 + 1))
                    if hl == 0 and r == 0:
                        t.wait_ge(s_x2[0].h, 256 * (tg + 1))
                    if hl == 0:
                        t.wait_ge(c_dev.h, tg * 128 + r * 32)
                    for gh in range(2):
                        for n in range(2):
                            eidx = tg * 128 + hcb * 2 + n
                            if gh == 0 and eidx >= 2:
                                t.wait_ge(c_guev.h, eidx - 1)
                            ins = None
                            for k in range(FC):
                                ins = t.matmul(
                                    ps_gu[:, gh * 1024 + n * 512:gh * 1024 + (n + 1) * 512],
                                    wguv[:, ci % 2, k, gh * P:(gh + 1) * P],
                                    xn2fv[:, k, n * 512:(n + 1) * 512],
                                    start=(k == 0), stop=(k == FC - 1))
                            if gh == 0:
                                c_gpe.inc(ins)
                            else:
                                c_upe.inc(ins)
                t.wait_ge(c_guev.h, tg * 128 + (r + 1) * 32)
                for fg in range(FG):
                    wj = tg * 32 + r * 8 + fg
                    t.wait_ge(s_wd[wj % 2].h, 16 * (wj // 2 + 1))
                    ins = None
                    for mc in range(2):
                        for n in range(2):
                            didx = tg * 128 + r * 32 + fg * 4 + mc * 2 + n
                            if didx >= 2:
                                t.wait_ge(c_dev.h, didx - 1)
                            for k in range(FC):
                                ins = t.matmul(
                                    ps_gu[:, (didx % 2) * 512:(didx % 2 + 1) * 512],
                                    wdv[:, wj % 2, k, mc * P:(mc + 1) * P],
                                    gatv[:, k, n * 512:(n + 1) * 512],
                                    start=(k == 0), stop=(k == FC - 1))
                            c_dpe.inc(ins)
            for i in range(8):
                gi = tg * 8 + i
                if i == 0:
                    t.wait_ge(c_dev.h, (tg + 1) * 128)
                for gr in range(4):
                    G = gi * 4 + gr
                    if G >= 2:
                        t.wait_ge(c_fev.h, G - 1)
                    ins = None
                    for j in range(4):
                        fc = gr * 4 + j
                        ins = t.transpose(
                            ps_tf[:, (G % 2) * 512 + j * P:(G % 2) * 512 + (j + 1) * P],
                            accv[:, fc, i * P:(i + 1) * P],
                            ident_f[:])
                    c_ft.inc(ins)

    @block.scalar
    def _(s):
        for tg in range(NTG):
            if tg == 0:
                for s_ in (s_spx[0], s_spx[1], s_spn[0], s_spn[1]):
                    s.wait_ge(s_.h, s_.n)
                s.wait_ge(c_ope.h, 64)
                s.wait_ge(c_ln2.h, 16)
            else:
                s.wait_ge(c_upe.h, 128)
            for fc in range(FC):
                s_x2[0].inc(s.dma_start(
                    out=xn2fv[:, fc, :],
                    in_=xn2_sp[tg * TTG:(tg + 1) * TTG, fc * P:(fc + 1) * P],
                    transpose=True), 16)
            for hcb in range(64):
                for n in range(2):
                    eidx = tg * 128 + hcb * 2 + n
                    s.wait_ge(c_gpe.h, eidx + 1)
                    if eidx >= 2:
                        s.wait_ge(c_guev.h, eidx - 1)
                    c_sil.inc(s.activation(
                        out=sg_tmp[:, n * 512:(n + 1) * 512],
                        in_=ps_gu[:, n * 512:(n + 1) * 512], func=AF.Silu))

    @block.vector
    def _(v):
        for tg in range(NTG):
            for r in range(NR):
                for hl in range(16):
                    hcb = r * 16 + hl
                    for n in range(2):
                        eidx = tg * 128 + hcb * 2 + n
                        v.wait_ge(c_sil.h, eidx + 1)
                        v.wait_ge(c_upe.h, eidx + 1)
                        if tg * 4 + r >= 1 and hl == 0 and n == 0:
                            v.wait_ge(c_dpe.h, tg * 128 + r * 32)
                        c_guev.inc(v.tensor_tensor(
                            out=gatv[:, hl, n * 512:(n + 1) * 512],
                            in0=sg_tmp[:, n * 512:(n + 1) * 512],
                            in1=ps_gu[:, 1024 + n * 512:1024 + (n + 1) * 512],
                            op=ALU.mult))
                for fg in range(FG):
                    for mc in range(2):
                        for n in range(2):
                            didx = tg * 128 + r * 32 + fg * 4 + mc * 2 + n
                            v.wait_ge(c_dpe.h, didx + 1)
                            fc = fg * 2 + mc
                            if r == 0:
                                if tg == 1 and fg == 0 and mc == 0 and n == 0:
                                    v.wait_ge(c_ft.h, 32)
                                c_dev.inc(v.tensor_copy(
                                    out=accv[:, fc, n * 512:(n + 1) * 512],
                                    in_=ps_gu[:, (didx % 2) * 512:(didx % 2 + 1) * 512]))
                            else:
                                c_dev.inc(v.tensor_tensor(
                                    out=accv[:, fc, n * 512:(n + 1) * 512],
                                    in0=ps_gu[:, (didx % 2) * 512:(didx % 2 + 1) * 512],
                                    in1=accv[:, fc, n * 512:(n + 1) * 512],
                                    op=ALU.add))
            for i in range(8):
                gi = tg * 8 + i
                v.wait_ge(s_x1r[gi % 2].h, 16 * (gi // 2 + 1))
                for gr in range(4):
                    G = gi * 4 + gr
                    v.wait_ge(c_ft.h, G + 1)
                    ins = v.tensor_tensor(
                        out=x1_sb4[:, (gi % 2) * HID + gr * 512:(gi % 2) * HID + (gr + 1) * 512],
                        in0=ps_tf[:, (G % 2) * 512:(G % 2 + 1) * 512],
                        in1=x1_sb4[:, (gi % 2) * HID + gr * 512:(gi % 2) * HID + (gr + 1) * 512],
                        op=ALU.add)
                    c_fev.inc(ins)

    p4.close()


def _prep(inputs):
    f32 = np.float32
    hs = np.asarray(inputs["L0_attn_hidden_states"], f32)
    Wq = np.asarray(inputs["L0_attn_Wq"], f32)
    bq = np.asarray(inputs["L0_attn_bq"], f32)
    Wk = np.asarray(inputs["L0_attn_Wk"], f32)
    bk = np.asarray(inputs["L0_attn_bk"], f32)
    Wv = np.asarray(inputs["L0_attn_Wv"], f32)
    bv = np.asarray(inputs["L0_attn_bv"], f32)
    Wo = np.asarray(inputs["L0_attn_Wo"], f32)
    Wg = np.asarray(inputs["L0_ffn_w_gate"], f32)
    Wu = np.asarray(inputs["L0_ffn_w_up"], f32)
    Wd = np.asarray(inputs["L0_ffn_w_down"], f32)

    wkv_t = np.concatenate([Wk, Wv], axis=0).T
    wkv_blk = np.ascontiguousarray(
        wkv_t.reshape(FC, P, 2 * KV).transpose(1, 0, 2).reshape(P, FC * 2 * KV))
    wq_blk = np.ascontiguousarray(
        Wq.T.reshape(FC, P, 8, 256).transpose(2, 1, 0, 3).reshape(8, P, FC * 256))
    wo_blk = np.ascontiguousarray(
        Wo.T.reshape(FC, P, 4, 512).transpose(2, 1, 0, 3).reshape(4, P, FC * 512))
    Gt = Wg.T.reshape(FC, P, 64, 128)
    Ut = Wu.T.reshape(FC, P, 64, 128)
    gu = np.concatenate([Gt, Ut], axis=3)
    wgu_blk = np.ascontiguousarray(gu.transpose(2, 1, 0, 3).reshape(64, P, FC * 256))
    WdT = np.ascontiguousarray(Wd.T)
    a = WdT.reshape(NR, FC, P, FG, 256).transpose(0, 3, 2, 1, 4)
    wd_blk = np.ascontiguousarray(a.reshape(NR * FG, P, FC * 256))

    bq_tile = np.ascontiguousarray(bq.reshape(FC, P).T)
    bkv_row = np.ascontiguousarray(np.concatenate([bk, bv]).reshape(1, 2 * KV))
    idb = np.eye(P, dtype=ml_dtypes.bfloat16)
    idf = np.eye(P, dtype=f32)
    sel8 = np.zeros((8, 1024), dtype=ml_dtypes.bfloat16)
    for j in range(8):
        sel8[j, j * 128:(j + 1) * 128] = 1

    common = dict(wkv_blk=wkv_blk, wq_blk=wq_blk, wo_blk=wo_blk, wgu_blk=wgu_blk,
                  wd_blk=wd_blk, bq_t=bq_tile, bkv_r=bkv_row, id_bf=idb, id_f32=idf,
                  sel8_d=sel8)
    in_maps = []
    for c in range(8):
        b, r0 = c // 4, (c % 4) * T
        m = dict(common)
        m["x_in"] = np.ascontiguousarray(hs[b, r0:r0 + T, :])
        in_maps.append(m)
    return in_maps


def kernel(**inputs):
    in_maps = _prep(inputs)
    nc, st = build_nc(debug=False)
    try:
        res = run_bass_kernel_spmd(nc, in_maps, core_ids=list(range(8)))
    finally:
        st.close()
    outp = np.empty((2, 8192, HID), np.float32)
    for c in range(8):
        b, r0 = c // 4, (c % 4) * T
        outp[b, r0:r0 + T, :] = res.results[c]["out"]
    return outp


# version 14
# speedup vs baseline: 1.0022x; 1.0022x over previous
import contextlib
import numpy as np
import ml_dtypes
import concourse.bass as bass
import concourse.mybir as mybir
from concourse.bass_utils import run_bass_kernel_spmd

F32 = mybir.dt.float32
BF16 = mybir.dt.bfloat16
AF = mybir.ActivationFunctionType
ALU = mybir.AluOpType
AX = mybir.AxisListType

P = 128
HID = 2048
T = 2048
FC = 16
TC = 16
HALF = 2
TCH = 8
TTH = 1024
KV = 512
KVH = 4
H = 16
LN_EPS = 1e-5
ATTN_EPS = 1e-6
NTG = 2
TTG = 1024
NR = 4
FG = 8
INV_N = 1.0 / HID

DEBUG_PHASE = 0
DEBUG_NO_CC = False


class Sem:
    def __init__(self, stack, nc, name):
        self.h = stack.enter_context(nc.semaphore(name))
        self.n = 0

    def inc(self, ins, k=1):
        ins.then_inc(self.h, k)
        self.n += k
        return ins

    def barrier(self, eng):
        eng.wait_ge(self.h, self.n)


def build_nc(debug=False):
    st = contextlib.ExitStack()
    nc = bass.Bass("TRN2", target_bir_lowering=False)

    x_in = nc.dram_tensor("x_in", [T, HID], F32, kind="ExternalInput").ap()
    wkv_blk = nc.dram_tensor("wkv_blk", [P, FC * 2 * KV], F32, kind="ExternalInput").ap()
    wq_blk = nc.dram_tensor("wq_blk", [8, P, FC * 256], F32, kind="ExternalInput").ap()
    wo_blk = nc.dram_tensor("wo_blk", [4, P, FC * 512], F32, kind="ExternalInput").ap()
    wgu_blk = nc.dram_tensor("wgu_blk", [64, P, FC * 256], F32, kind="ExternalInput").ap()
    wd_blk = nc.dram_tensor("wd_blk", [NR * FG, P, FC * 256], F32, kind="ExternalInput").ap()
    bq_t = nc.dram_tensor("bq_t", [P, FC], F32, kind="ExternalInput").ap()
    bkv_r = nc.dram_tensor("bkv_r", [1, 2 * KV], F32, kind="ExternalInput").ap()
    id_bf = nc.dram_tensor("id_bf", [P, P], BF16, kind="ExternalInput").ap()
    id_f32 = nc.dram_tensor("id_f32", [P, P], F32, kind="ExternalInput").ap()
    sel8_d = nc.dram_tensor("sel8_d", [8, 1024], BF16, kind="ExternalInput").ap()
    out = nc.dram_tensor("out", [T, HID], F32, kind="ExternalOutput").ap()

    x1_sp = nc.dram_tensor("x1_sp", [T, HID], F32).ap()
    xn2_sp = nc.dram_tensor("xn2_sp", [T, HID], BF16).ap()
    cc_in_t = nc.dram_tensor("cc_in", [P, KVH * 130], BF16)
    cc_out_t = nc.dram_tensor("cc_out", [P, KVH * 130], BF16)

    dbg = {}
    if debug:
        dbg["d_sq"] = nc.dram_tensor("d_sq", [P, H * T], BF16, kind="ExternalOutput").ap()
        dbg["d_kvf"] = nc.dram_tensor("d_kvf", [P, KVH * 130], F32, kind="ExternalOutput").ap()
        if DEBUG_PHASE != 1:
            dbg["d_x1"] = nc.dram_tensor("d_x1", [T, HID], F32, kind="ExternalOutput").ap()
            dbg["d_xn2"] = nc.dram_tensor("d_xn2", [T, HID], BF16, kind="ExternalOutput").ap()

    Sm = lambda name: Sem(st, nc, name)
    E = {}
    for name in ["s_c", "s_x0", "s_x1", "s_wkv", "s_wq0", "s_wq1", "s_wo",
                 "s_wo1", "s_wo2", "s_wo3", "s_wq2", "s_wgu0", "s_wgu1", "s_wd0", "s_wd1", "s_cc",
                 "s_x30", "s_x31", "s_spx0", "s_spx1", "s_spn0", "s_spn1",
                 "s_x20", "s_x21", "s_x1r0", "s_x1r1", "s_out0", "s_out1", "c_bias",
                 "c_red", "c_sq", "c_var", "c_rstd", "c_negmr", "c_ln",
                 "c_tr", "c_xn1", "c_kvp", "c_pk", "c_e", "c_skv", "c_ka",
                 "c_ccp", "c_kvbf", "c_qpe", "c_qact", "c_qev",
                 "c_n3a", "c_nact", "c_b3a", "c_inv", "c_q3a", "c_attn",
                 "c_ope", "c_x1e", "c2_red", "c2_sq", "c2_var", "c2_rstd",
                 "c2_negmr", "c_ln2", "c_x2t", "c_x2f",
                 "c_gpe", "c_upe", "c_sil", "c_guev",
                 "c_dpe", "c_dev", "c_dwd", "c_ddone", "c_ft", "c_fev", "c_fin",
                 "c_dv", "c_ac", "c_rowp", "c_row", "c_qb3", "c_qbc", "c_stg", "c_sqp", "c_sqs", "c_qln"]:
        E[name] = Sm(name)
    E["s_wol"] = [E["s_wo"], E["s_wo1"], E["s_wo2"], E["s_wo3"]]
    E["s_x"] = [E["s_x0"], E["s_x1"]]
    E["s_wq"] = [E["s_wq0"], E["s_wq1"], E["s_wq2"]]
    E["s_wgu"] = [E["s_wgu0"], E["s_wgu1"]]
    E["s_wd"] = [E["s_wd0"], E["s_wd1"]]
    E["s_x3"] = [E["s_x30"], E["s_x31"]]
    E["s_spx"] = [E["s_spx0"], E["s_spx1"]]
    E["s_spn"] = [E["s_spn0"], E["s_spn1"]]
    E["s_x2"] = [E["s_x20"], E["s_x21"]]
    E["s_x1r"] = [E["s_x1r0"], E["s_x1r1"]]
    E["s_out"] = [E["s_out0"], E["s_out1"]]
    E.update(dict(x_in=x_in, wkv_blk=wkv_blk, wq_blk=wq_blk, wo_blk=wo_blk,
                  wgu_blk=wgu_blk, wd_blk=wd_blk, bq_t=bq_t, bkv_r=bkv_r,
                  id_bf=id_bf, id_f32=id_f32, sel8_d=sel8_d, out=out,
                  x1_sp=x1_sp, xn2_sp=xn2_sp,
                  cc_in_t=cc_in_t, cc_out_t=cc_out_t, debug=debug, dbg=dbg))

    with (
        nc.sbuf_tensor("ident_bf", [P, P], BF16) as ident_bf,
        nc.sbuf_tensor("ident_f", [P, P], F32) as ident_f,
        nc.sbuf_tensor("ones_row_f", [1, P], F32) as ones_row_f,
        nc.sbuf_tensor("ones_col_bf", [P, 1], BF16) as ones_col_bf,
        nc.sbuf_tensor("ones_row_bf", [1, P], BF16) as ones_row_bf,
        nc.sbuf_tensor("sel8_sb", [8, 1024], BF16) as sel8_sb,
        nc.sbuf_tensor("bq_sb", [P, FC], F32) as bq_sb,
        nc.sbuf_tensor("negbq_sb", [P, FC], F32) as negbq_sb,
        nc.sbuf_tensor("bq1_sb", [P, FC], F32) as bq1_sb,
        nc.sbuf_tensor("bkv_sb", [1, 2 * KV], F32) as bkv_sb,
        nc.sbuf_tensor("bk1_bc", [P, KV], F32) as bk1_bc,
        nc.sbuf_tensor("bv_bc", [P, KV], F32) as bv_bc,
        nc.sbuf_tensor("stat", [P, 24], F32) as stat,
        nc.sbuf_tensor("eps_ln", [P, 1], F32) as eps_ln,
        nc.sbuf_tensor("eps_at", [P, 1], F32) as eps_at,
        nc.sbuf_tensor("kv_bf", [P, KVH * P], BF16) as kv_bf,
        nc.sbuf_tensor("ksum_bf", [P, KVH], BF16) as ksum_bf,
        nc.sbuf_tensor("kvf_sb", [P, KVH * 130], BF16) as kvf_sb,
        nc.sbuf_tensor("cc_sb", [P, KVH * 130], BF16) as cc_sb,
        nc.Block() as block,
    ):
        E.update(dict(ident_bf=ident_bf, ident_f=ident_f, ones_row_f=ones_row_f,
                      ones_col_bf=ones_col_bf, ones_row_bf=ones_row_bf, sel8_sb=sel8_sb,
                      bq_sb=bq_sb, negbq_sb=negbq_sb, bq1_sb=bq1_sb, bkv_sb=bkv_sb,
                      bk1_bc=bk1_bc, bv_bc=bv_bc, stat=stat, kv_bf=kv_bf,
                      eps_ln=eps_ln, eps_at=eps_at,
                      ksum_bf=ksum_bf, kvf_sb=kvf_sb, cc_sb=cc_sb))
        s_c = E["s_c"]; c_bias = E["c_bias"]; s_cc = E["s_cc"]

        @block.sync
        def _(sync):
            s_c.inc(sync.dma_start(out=ident_bf[:], in_=id_bf), 16)
            s_c.inc(sync.dma_start(out=ident_f[:], in_=id_f32), 16)
            s_c.inc(sync.dma_start(out=bq_sb[:], in_=bq_t), 16)
            s_c.inc(sync.dma_start(out=bkv_sb[:], in_=bkv_r), 16)
            s_c.inc(sync.dma_start(out=sel8_sb[:], in_=sel8_d), 16)

        @block.vector
        def _(v):
            v.wait_ge(s_c.h, 80)
            v.memset(ones_row_f[:], 1.0)
            v.memset(ones_col_bf[:], 1.0)
            v.memset(ones_row_bf[:], 1.0)
            v.memset(eps_ln[:], LN_EPS)
            v.memset(eps_at[:], ATTN_EPS)
            v.tensor_scalar(out=negbq_sb[:], in0=bq_sb[:], scalar1=-1.0, scalar2=None, op0=ALU.mult)
            c_bias.inc(v.tensor_scalar(out=bq1_sb[:], in0=bq_sb[:], scalar1=1.0, scalar2=None, op0=ALU.add))

        with nc.psum_tensor("ps_bias", [P, 2 * KV], F32) as ps_bias:
            @block.tensor
            def _(t):
                t.wait_ge(c_bias.h, 1)
                t.matmul(ps_bias[:, 0:KV], ones_row_f[:], bkv_sb[:, 0:KV], start=True, stop=True)
                c_bias.inc(t.matmul(ps_bias[:, KV:2 * KV], ones_row_f[:], bkv_sb[:, KV:2 * KV],
                                    start=True, stop=True))

            @block.vector
            def _(v):
                v.wait_ge(c_bias.h, 2)
                v.tensor_scalar(out=bk1_bc[:], in0=ps_bias[:, 0:KV], scalar1=1.0, scalar2=None, op0=ALU.add)
                c_bias.inc(v.tensor_copy(out=bv_bc[:], in_=ps_bias[:, KV:2 * KV]))

        with nc.sbuf_tensor("sq_sb", [P, H * T], BF16) as sq_sb:
            E["sq_sb"] = sq_sb
            _p1(nc, block, E)
            if DEBUG_PHASE == 1:
                @block.sync
                def _(sync):
                    so = E["s_out"][0]
                    sync.wait_ge(E["c_qev"].h, 64)
                    sync.wait_ge(E["c_kvbf"].h, 1)
                    so.inc(sync.dma_start(out=dbg["d_sq"], in_=sq_sb[:]), 16)
                    so.inc(sync.dma_start(out=dbg["d_kvf"], in_=kvf_sb[:]), 16)
                    sync.wait_ge(so.h, so.n)
                return nc, st
            _p3(nc, block, E)

        if DEBUG_PHASE == 3:
            @block.sync
            def _(sync):
                for s in (E["s_spx0"], E["s_spx1"], E["s_spn0"], E["s_spn1"]):
                    sync.wait_ge(s.h, s.n)
                if debug:
                    so = E["s_out0"]
                    so.inc(sync.dma_start(out=dbg["d_x1"], in_=x1_sp), 16)
                    so.inc(sync.dma_start(out=dbg["d_xn2"], in_=xn2_sp), 16)
                    sync.wait_ge(so.h, so.n)
            return nc, st

        _p4(nc, block, E)

        @block.sync
        def _(sync):
            sync.wait_ge(E["s_out0"].h, E["s_out0"].n)
            sync.wait_ge(E["s_out1"].h, E["s_out1"].n)

    return nc, st


def _p1(nc, block, E):
    x_in = E["x_in"]; sq_sb = E["sq_sb"]; stat = E["stat"]
    ident_bf = E["ident_bf"]; ident_f = E["ident_f"]; ones_row_f = E["ones_row_f"]
    ones_col_bf = E["ones_col_bf"]; sel8_sb = E["sel8_sb"]; ones_row_bf = E["ones_row_bf"]
    bk1_bc = E["bk1_bc"]; bv_bc = E["bv_bc"]; negbq_sb = E["negbq_sb"]; bq1_sb = E["bq1_sb"]
    wkv_blk = E["wkv_blk"]; wq_blk = E["wq_blk"]
    cc_sb = E["cc_sb"]; kvf_sb = E["kvf_sb"]; kv_bf = E["kv_bf"]; ksum_bf = E["ksum_bf"]
    cc_in_t = E["cc_in_t"]; cc_out_t = E["cc_out_t"]
    s_x = E["s_x"]; s_wkv = E["s_wkv"]; s_wkv2 = E["c_dwd"]; s_wq = E["s_wq"]; s_cc = E["s_cc"]
    c_bias = E["c_bias"]
    c_red = E["c_red"]; c_sq = E["c_sq"]; c_var = E["c_var"]; c_rstd = E["c_rstd"]
    c_negmr = E["c_negmr"]; c_ln = E["c_ln"]; c_tr = E["c_tr"]; c_xn1 = E["c_xn1"]
    c_kvp = E["c_kvp"]; c_pk = E["c_pk"]; c_e = E["c_e"]; c_skv = E["c_skv"]
    c_ka = E["c_ka"]; c_ccp = E["c_ccp"]; c_kvbf = E["c_kvbf"]
    c_qpe = E["c_qpe"]; c_qact = E["c_qact"]; c_qev = E["c_qev"]
    c_dv = E["c_dv"]; c_ac = E["c_ac"]
    c_rowp = E["c_rowp"]; c_row = E["c_row"]; c_qb3 = E["c_qb3"]; c_qbc = E["c_qbc"]
    c_stg = E["c_stg"]; c_sqp = E["c_sqp"]; c_sqs = E["c_sqs"]; c_qln = E["c_qln"]
    eps_ln = E["eps_ln"]

    p1 = contextlib.ExitStack()
    sb = lambda *a: p1.enter_context(nc.sbuf_tensor(*a))
    pt = lambda *a: p1.enter_context(nc.psum_tensor(*a))
    xn1_sb = sb("xn1_sb", [P, FC * TTH], BF16)
    wkv_sb = sb("wkv_sb", [P, FC * 2 * KV], BF16)
    wq_sb = sb("wq_sb", [P, 2 * FC * 256], BF16)
    x_sb = sb("x_sb", [P, 2 * HID], F32)
    xnt_sb = sb("xnt_sb", [P, 2 * HID], BF16)
    sk_sb = sb("sk_sb", [P, 2 * KV], BF16)
    v_sb = sb("v_sb", [P, 2 * KVH * 130], BF16)
    pk_sb = sb("pk_sb", [P, 2 * KV], BF16)
    e_q = sb("e_q", [P, 4 * 512], F32)
    skrow = sb("skrow", [1, 1024], BF16)
    skb_bc = sb("skb_bc", [P, KV], BF16)
    svb_bc = sb("svb_bc", [P, KV], BF16)
    sq_cs = sb("sq_cs", [P, FC], F32)
    stg_r = sb("stg_r", [P, 8], F32)
    stg_nm = sb("stg_nm", [P, 8], F32)
    rows_r = sb("rows_r", [8, P], BF16)
    rows_nm = sb("rows_nm", [8, P], BF16)
    rq_bc = sb("rq_bc", [P, 1024], BF16)
    nm_bc = sb("nm_bc", [P, 1024], BF16)
    ps_tr = pt("ps_tr", [P, 2048], BF16)
    ps_big = pt("ps_big", [P, 2048], F32)
    ps_ka0 = pt("ps_ka0", [P, 512], F32)
    ps_ka1 = pt("ps_ka1", [P, 512], F32)
    E["_p1_stack"] = p1

    xn1v = xn1_sb[:].rearrange("p (f t) -> p f t", f=FC)
    wkvv = wkv_sb[:].rearrange("p (k m) -> p k m", k=FC)
    wqv = wq_sb[:].rearrange("p (b k m) -> p b k m", b=2, k=FC)
    ps_ka = [ps_ka0, ps_ka1]

    @block.sync
    def _(sync):
        for gt in range(TC):
            if gt >= 2:
                sync.wait_ge(c_sq.h, gt - 1)
                sync.wait_ge(c_red.h, gt - 1)
            s_x[gt % 2].inc(sync.dma_start(
                out=x_sb[:, (gt % 2) * HID:(gt % 2 + 1) * HID],
                in_=x_in[gt * P:(gt + 1) * P, :]), 16)
        sync.wait_ge(c_ccp.h, 1)
        s_cc.inc(sync.dma_start(out=cc_in_t.ap(), in_=cc_sb[:]), 16)
        sync.wait_ge(s_cc.h, 17)
        s_cc.inc(sync.dma_start(out=kvf_sb[:], in_=cc_out_t.ap()), 16)

    @block.gpsimd
    def _(gp):
        s_wkv.inc(gp.dma_start(out=wkv_sb[:, 0:FC * 512], in_=wkv_blk[:, 0:FC * 512]), 16)
        s_wkv2.inc(gp.dma_start(out=wkv_sb[:, FC * 512:FC * 1024], in_=wkv_blk[:, FC * 512:FC * 1024]), 16)
        for ci in range(8):
            if ci >= 2:
                gp.wait_ge(c_qpe.h, 4 * (ci - 1))
            s_wq[ci % 2].inc(gp.dma_start(
                out=wq_sb[:, (ci % 2) * FC * 256:(ci % 2 + 1) * FC * 256],
                in_=wq_blk[ci % 8, :, :]), 16)
        gp.wait_ge(s_cc.h, 16)
        if DEBUG_NO_CC:
            s_cc.inc(gp.dma_start(out=cc_out_t.ap(), in_=cc_in_t.ap()), 16)
        else:
            s_cc.inc(gp.collective_compute(
                "AllReduce", ALU.add,
                replica_groups=[[0, 1, 2, 3], [4, 5, 6, 7]],
                ins=[cc_in_t.ap().opt()],
                outs=[cc_out_t.ap().opt()]), 1)
        for ci in range(8, 16):
            gp.wait_ge(c_qpe.h, 4 * (ci - 1))
            s_wq[ci % 2].inc(gp.dma_start(
                out=wq_sb[:, (ci % 2) * FC * 256:(ci % 2 + 1) * FC * 256],
                in_=wq_blk[ci % 8, :, :]), 16)

    @block.tensor
    def _(t):
        t.wait_ge(c_bias.h, 4)
        ins = None
        for nn in range(2):
            for k in range(FC):
                if nn == 0 and k == 0:
                    t.wait_ge(s_wkv.h, 16)
                if nn == 0 and k == 8:
                    t.wait_ge(s_wkv2.h, 16)
                ins = t.matmul(ps_big[0:1, nn * 512:(nn + 1) * 512],
                               ones_col_bf[:], wkvv[:, k, nn * 512:(nn + 1) * 512],
                               start=(k == 0), stop=(k == FC - 1))
        c_bias.inc(ins)
        t.wait_ge(c_bias.h, 6)
        t.matmul(ps_big[:, 0:512], ones_row_bf[:],
                 skrow[:, 0:512], start=True, stop=True)
        c_bias.inc(t.matmul(ps_big[:, 512:1024], ones_row_bf[:],
                            skrow[:, 512:1024], start=True, stop=True))

        def transp(gt):
            par = gt % 2
            t.wait_ge(c_ln.h, gt + 1)
            for gr in range(4):
                G = gt * 4 + gr
                if G >= 2:
                    t.wait_ge(c_xn1.h, G - 1)
                ins = None
                for j in range(4):
                    fc = gr * 4 + j
                    ins = t.transpose(
                        ps_tr[:, (gr % 2) * 1024 + j * P:(gr % 2) * 1024 + (j + 1) * P],
                        xnt_sb[:, par * HID + fc * P:par * HID + (fc + 1) * P],
                        ident_bf[:])
                c_tr.inc(ins)

        transp(0)
        for half in range(HALF):
            for i in range(TCH):
                gt = half * TCH + i
                par = gt % 2
                if half == 1 and i == 0:
                    transp(gt)
                if gt + 1 < TC and (gt + 1) % TCH != 0:
                    transp(gt + 1)
                t.wait_ge(c_xn1.h, 4 * (gt + 1))
                if gt == 0:
                    t.wait_ge(c_bias.h, 8)
                if gt >= 2:
                    t.wait_ge(c_skv.h, gt - 1)
                if half == 1 and i == 0:
                    t.wait_ge(c_qev.h, 32)
                ins = None
                for nn in range(2):
                    for k in range(FC):
                        ins = t.matmul(
                            ps_big[:, par * 1024 + nn * 512:par * 1024 + (nn + 1) * 512],
                            xn1v[:, k, i * P:(i + 1) * P],
                            wkvv[:, k, nn * 512:(nn + 1) * 512],
                            start=(k == 0), stop=(k == FC - 1))
                c_kvp.inc(ins)
                for kprev in ([gt - 1] if gt >= 1 else []) + ([gt] if gt == TC - 1 else []):
                    kpar = kprev % 2
                    t.wait_ge(c_skv.h, kprev + 1)
                    ins = None
                    for kvh in range(KVH):
                        ins = t.matmul(
                            ps_ka[kvh // 2][:, (kvh % 2) * 130:(kvh % 2) * 130 + 129],
                            sk_sb[:, kpar * KV + kvh * P:kpar * KV + (kvh + 1) * P],
                            v_sb[:, kpar * KVH * 130 + kvh * 130:kpar * KVH * 130 + kvh * 130 + 129],
                            start=False, stop=(kprev == TC - 1), skip_group_check=True)
                    c_ka.inc(ins)
            t.wait_ge(c_skv.h, (half + 1) * TCH - 1)
            t.wait_ge(c_stg.h, (half + 1) * TCH)
            t.transpose(ps_big[0:8, 0:P], stg_r[:, 0:8], ident_f[:])
            c_rowp.inc(t.transpose(ps_big[0:8, 512:512 + P],
                                   stg_nm[:, 0:8], ident_f[:]))
            t.wait_ge(c_row.h, half + 1)
            t.wait_ge(c_skv.h, (half + 1) * TCH)
            ins = None
            for j in range(8):
                n = j // 4
                ins = t.matmul(ps_big[:, n * 512 + (j % 4) * P:n * 512 + (j % 4 + 1) * P],
                               sel8_sb[:, j * P:(j + 1) * P], rows_r[:],
                               start=True, stop=True, skip_group_check=True)
                ins = t.matmul(ps_big[:, 1024 + n * 512 + (j % 4) * P:1024 + n * 512 + (j % 4 + 1) * P],
                               sel8_sb[:, j * P:(j + 1) * P], rows_nm[:],
                               start=True, stop=True, skip_group_check=True)
            c_qb3.inc(ins)
            for qg in range(8):
                ci = half * 8 + qg
                t.wait_ge(s_wq[ci % 2].h, 16 * (ci // 2 + 1))
                if qg == 0:
                    t.wait_ge(c_qbc.h, half + 1)
                if half == 0:
                    t.wait_ge(c_sqs.h, ci)
                    ins = None
                    for hh in range(2):
                        for k in range(FC):
                            ins = t.matmul(ps_ka1[:, 384 + hh:385 + hh],
                                           wqv[:, ci % 2, k, hh * P:(hh + 1) * P],
                                           ones_col_bf[:],
                                           start=False, stop=True, skip_group_check=True)
                    c_sqp.inc(ins)
                for hh in range(2):
                    for n in range(2):
                        idx = ci * 4 + hh * 2 + n
                        if idx >= 4:
                            t.wait_ge(c_qev.h, idx - 3)
                        ins = None
                        for k in range(FC):
                            ins = t.matmul(
                                ps_big[:, (idx % 4) * 512:(idx % 4 + 1) * 512],
                                wqv[:, ci % 2, k, hh * P:(hh + 1) * P],
                                xn1v[:, k, n * 512:(n + 1) * 512],
                                start=(k == 0), stop=(k == FC - 1))
                        c_qpe.inc(ins)

    @block.scalar
    def _(s):
        def cast_act(gt):
            par = gt % 2
            xt = x_sb[:, par * HID:(par + 1) * HID]
            s.wait_ge(s_x[gt % 2].h, 16 * (gt // 2 + 1))
            if gt >= 2:
                s.wait_ge(c_tr.h, 4 * gt - 4)
            c_ln.inc(s.activation(out=xnt_sb[:, par * HID:(par + 1) * HID], in_=xt,
                                  func=AF.Identity))

        def sq_act(gt):
            par = gt % 2
            sc = par * 12
            xt = x_sb[:, par * HID:(par + 1) * HID]
            s.wait_ge(c_red.h, gt + 1)
            s.wait_ge(c_ln.h, gt + 1)
            c_sq.inc(s.activation(out=xt, in_=xt, func=AF.Square,
                                  accum_out=stat[:, sc + 1:sc + 2]))

        def lnexp_act(gt):
            sc = (gt % 2) * 12
            s.wait_ge(c_var.h, gt + 1)
            c_ac.inc(s.activation(out=stat[:, sc + 6:sc + 7], in_=stat[:, sc + 5:sc + 6],
                                  func=AF.Ln, bias=eps_ln[:]))
            c_ac.barrier(s)
            c_rstd.inc(s.activation(out=stat[:, sc + 7:sc + 8], in_=stat[:, sc + 6:sc + 7],
                                    func=AF.Exp, scale=-0.5))
            c_rstd.barrier(s)

        cast_act(0)
        sq_act(0)
        cast_act(1)
        sq_act(1)
        lnexp_act(0)
        for half in range(HALF):
            for i in range(TCH):
                gt = half * TCH + i
                par = gt % 2
                if gt + 2 < TC:
                    cast_act(gt + 2)
                    sq_act(gt + 2)
                if gt + 1 < TC:
                    lnexp_act(gt + 1)
                s.wait_ge(c_pk.h, gt + 1)
                pk = pk_sb[:, par * KV:(par + 1) * KV]
                ek = e_q[:, par * 512:(par + 1) * 512]
                c_ac.inc(s.activation(out=ek, in_=pk, func=AF.Relu, scale=-1.0, bias=1.0))
                c_ac.barrier(s)
                c_e.inc(s.activation(out=ek, in_=ek, func=AF.Exp, scale=-1.0))
            for qg in range(8):
                ci = half * 8 + qg
                for hh in range(2):
                    qc = qg * 2 + hh
                    for n in range(2):
                        idx = ci * 4 + hh * 2 + n
                        s.wait_ge(c_qln.h, idx + 1)
                        if idx >= 2:
                            s.wait_ge(c_qev.h, idx - 1)
                        t1s = e_q[:, (idx % 2) * 512:(idx % 2 + 1) * 512]
                        ers = e_q[:, 1024 + (idx % 2) * 512:1024 + (idx % 2 + 1) * 512]
                        c_ac.inc(s.activation(out=ers, in_=t1s, func=AF.Relu, scale=-1.0,
                                              bias=negbq_sb[:, qc:qc + 1]))
                        c_ac.barrier(s)
                        c_qact.inc(s.activation(out=ers, in_=ers, func=AF.Exp, scale=-1.0))

    @block.vector
    def _(v):
        v.memset(ps_ka0[:], 0.0)
        v.memset(ps_ka1[:], 0.0)
        v.wait_ge(c_bias.h, 2)
        v.memset(cc_sb[:], 0.0)
        c_bias.inc(v.memset(
            v_sb[:].rearrange("p (g c) -> p g c", c=130)[:, :, 128:130], 1.0))
        v.wait_ge(c_bias.h, 5)
        c_bias.inc(v.tensor_copy(out=skrow[:], in_=ps_big[0:1, 0:1024]))
        v.wait_ge(c_bias.h, 7)
        v.tensor_copy(out=skb_bc[:], in_=ps_big[:, 0:512])
        c_bias.inc(v.tensor_copy(out=svb_bc[:], in_=ps_big[:, 512:1024]))

        def ln_dve(gt):
            par = gt % 2
            sc = par * 12
            xt = x_sb[:, par * HID:(par + 1) * HID]
            v.wait_ge(s_x[gt % 2].h, 16 * (gt // 2 + 1))
            c_red.inc(v.reduce_sum(out=stat[:, sc + 0:sc + 1], in_=xt, axis=AX.X))
            c_red.barrier(v)

        def chain_dve(gt):
            par = gt % 2
            sc = par * 12
            li = gt % TCH
            v.wait_ge(c_sq.h, gt + 1)
            c_dv.inc(v.tensor_scalar(out=stat[:, sc + 2:sc + 3], in0=stat[:, sc + 0:sc + 1],
                                     scalar1=INV_N, scalar2=None, op0=ALU.mult))
            c_dv.barrier(v)
            c_dv.inc(v.tensor_tensor(out=stat[:, sc + 4:sc + 5], in0=stat[:, sc + 2:sc + 3],
                                     in1=stat[:, sc + 2:sc + 3], op=ALU.mult))
            c_dv.barrier(v)
            c_var.inc(v.tensor_scalar(out=stat[:, sc + 5:sc + 6], in0=stat[:, sc + 1:sc + 2],
                                      scalar1=INV_N, scalar2=stat[:, sc + 4:sc + 5],
                                      op0=ALU.mult, op1=ALU.subtract))
            v.wait_ge(c_rstd.h, gt + 1)
            c_negmr.inc(v.tensor_scalar(out=stat[:, sc + 8:sc + 9], in0=stat[:, sc + 2:sc + 3],
                                        scalar1=stat[:, sc + 7:sc + 8], scalar2=-1.0,
                                        op0=ALU.mult, op1=ALU.mult))
            c_negmr.barrier(v)
            if gt == TCH:
                v.wait_ge(c_rowp.h, 1)
            v.tensor_copy(out=stg_r[:, li:li + 1], in_=stat[:, sc + 7:sc + 8])
            c_stg.inc(v.tensor_copy(out=stg_nm[:, li:li + 1], in_=stat[:, sc + 8:sc + 9]))

        def xn1_evac(gt):
            li = gt % TCH
            if gt == TCH:
                v.wait_ge(c_qpe.h, 32)
            for gr in range(4):
                G = gt * 4 + gr
                v.wait_ge(c_tr.h, G + 1)
                c_xn1.inc(v.tensor_copy(
                    out=xn1v[:, gr * 4:(gr + 1) * 4, li * P:(li + 1) * P],
                    in_=ps_tr[:, (gr % 2) * 1024:(gr % 2) * 1024 + 512]
                    .rearrange("p (a b) -> p a b", a=4)))

        ln_dve(0)
        ln_dve(1)
        chain_dve(0)
        xn1_evac(0)
        for half in range(HALF):
            for i in range(TCH):
                gt = half * TCH + i
                par = gt % 2
                sc = par * 12
                if half == 1 and i == 0:
                    xn1_evac(gt)
                    if gt + 1 < TC:
                        xn1_evac(gt + 1)
                    if gt + 2 < TC:
                        ln_dve(gt + 2)
                    if gt + 1 < TC:
                        chain_dve(gt + 1)
                else:
                    if gt + 1 < TC and (gt + 1) % TCH != 0:
                        xn1_evac(gt + 1)
                    if gt + 2 < TC:
                        ln_dve(gt + 2)
                    if gt + 1 < TC:
                        chain_dve(gt + 1)
                v.wait_ge(c_kvp.h, gt + 1)
                if gt >= 2:
                    v.wait_ge(c_e.h, gt - 1)
                    v.wait_ge(c_ka.h, gt - 1)
                kreg = ps_big[:, par * 1024:par * 1024 + 512]
                vreg = ps_big[:, par * 1024 + 512:par * 1024 + 1024]
                basek = e_q[:, 1024:1536]
                basev = e_q[:, 1536:2048]
                v.wait_ge(c_skv.h, gt)
                v.scalar_tensor_tensor(out=basek, in0=skb_bc[:], scalar=stat[:, sc + 8:sc + 9],
                                       in1=bk1_bc[:], op0=ALU.mult, op1=ALU.add)
                c_dv.inc(v.scalar_tensor_tensor(out=basev, in0=svb_bc[:],
                                                scalar=stat[:, sc + 8:sc + 9],
                                                in1=bv_bc[:], op0=ALU.mult, op1=ALU.add))
                c_dv.barrier(v)
                c_pk.inc(v.scalar_tensor_tensor(out=pk_sb[:, par * KV:(par + 1) * KV],
                                                in0=kreg, scalar=stat[:, sc + 7:sc + 8],
                                                in1=basek, op0=ALU.mult, op1=ALU.add))
                v.wait_ge(c_e.h, gt + 1)
                v.tensor_tensor(out=sk_sb[:, par * KV:(par + 1) * KV],
                                in0=pk_sb[:, par * KV:(par + 1) * KV],
                                in1=e_q[:, par * 512:(par + 1) * 512], op=ALU.max)
                c_skv.inc(v.scalar_tensor_tensor(
                    out=v_sb[:, par * KVH * 130:(par + 1) * KVH * 130]
                    .rearrange("p (g c) -> p g c", g=KVH)[:, :, 0:128],
                    in0=vreg, scalar=stat[:, sc + 7:sc + 8],
                    in1=basev, op0=ALU.mult, op1=ALU.add))
            if half == 1:
                v.wait_ge(c_ka.h, TC)
                ins = None
                for kvh in range(KVH):
                    ins = v.tensor_copy(
                        out=cc_sb[:, kvh * 130:kvh * 130 + 129],
                        in_=ps_ka[kvh // 2][:, (kvh % 2) * 130:(kvh % 2) * 130 + 129])
                c_ccp.inc(ins)
            v.wait_ge(c_rowp.h, half + 1)
            v.tensor_copy(out=rows_r[:], in_=ps_big[0:8, 0:P])
            c_row.inc(v.tensor_copy(out=rows_nm[:], in_=ps_big[0:8, 512:512 + P]))
            v.wait_ge(c_qb3.h, half + 1)
            v.tensor_copy(out=rq_bc[:, 0:512], in_=ps_big[:, 0:512])
            v.tensor_copy(out=rq_bc[:, 512:1024], in_=ps_big[:, 512:1024])
            v.tensor_copy(out=nm_bc[:, 0:512], in_=ps_big[:, 1024:1536])
            c_qbc.inc(v.tensor_copy(out=nm_bc[:, 512:1024], in_=ps_big[:, 1536:2048]))
            for qg in range(8):
                ci = half * 8 + qg
                qc0 = qg * 2
                if half == 0:
                    v.wait_ge(c_sqp.h, ci + 1)
                    c_dv.inc(v.tensor_copy(out=sq_cs[:, qc0:qc0 + 2], in_=ps_ka1[:, 384:386]))
                    c_dv.barrier(v)
                    c_sqs.inc(v.memset(ps_ka1[:, 384:386], 0.0))
                def q_max(idx):
                    qc_ = (idx % 32) // 4 * 2 + (idx % 4) // 2
                    qcq = ((idx // 4) % 8) * 2 + (idx % 4) // 2
                    n_ = idx % 2
                    hf = idx // 32
                    t1s = e_q[:, (idx % 2) * 512:(idx % 2 + 1) * 512]
                    ers = e_q[:, 1024 + (idx % 2) * 512:1024 + (idx % 2 + 1) * 512]
                    v.wait_ge(c_qact.h, idx + 1)
                    col = qcq * T + hf * TTH + n_ * 512
                    c_qev.inc(v.scalar_tensor_tensor(
                        out=sq_sb[:, col:col + 512], in0=t1s,
                        scalar=bq1_sb[:, qcq:qcq + 1], in1=ers,
                        op0=ALU.add, op1=ALU.max))

                for hh in range(2):
                    qc = qg * 2 + hh
                    for n in range(2):
                        idx = ci * 4 + hh * 2 + n
                        if idx % 32 != 0:
                            q_max(idx - 1)
                        v.wait_ge(c_qpe.h, idx + 1)
                        if idx >= 2:
                            v.wait_ge(c_qev.h, idx - 1)
                        t1s = e_q[:, (idx % 2) * 512:(idx % 2 + 1) * 512]
                        reg = ps_big[:, (idx % 4) * 512:(idx % 4 + 1) * 512]
                        c_dv.inc(v.tensor_tensor(out=t1s, in0=reg,
                                 in1=rq_bc[:, n * 512:(n + 1) * 512], op=ALU.mult))
                        c_dv.barrier(v)
                        c_qln.inc(v.scalar_tensor_tensor(
                            out=t1s, in0=nm_bc[:, n * 512:(n + 1) * 512],
                            scalar=sq_cs[:, qc:qc + 1], in1=t1s,
                            op0=ALU.mult, op1=ALU.add))
            if True:
                q_max(half * 32 + 31)
        v.wait_ge(s_cc.h, 33)
        for kvh in range(KVH):
            v.tensor_copy(out=kv_bf[:, kvh * P:(kvh + 1) * P],
                          in_=kvf_sb[:, kvh * 130:kvh * 130 + 128])
        c_kvbf.inc(v.tensor_copy(
            out=ksum_bf[:],
            in_=kvf_sb[:].rearrange("p (g c) -> p g c", g=KVH)[:, :, 128:129]
            .rearrange("p g c -> p (g c)")))

    p1.close()


def _p3(nc, block, E):
    x_in = E["x_in"]; sq_sb = E["sq_sb"]; stat = E["stat"]
    ones_row_f = E["ones_row_f"]; kv_bf = E["kv_bf"]; ksum_bf = E["ksum_bf"]
    wo_blk = E["wo_blk"]; x1_sp = E["x1_sp"]; xn2_sp = E["xn2_sp"]
    debug = E["debug"]; dbg = E["dbg"]
    s_wo = E["s_wo"]; s_x3 = E["s_x3"]; s_spx = E["s_spx"]; s_spn = E["s_spn"]
    s_cc = E["s_cc"]
    c_qpe = E["c_qpe"]; c_qev = E["c_qev"]; c_kvbf = E["c_kvbf"]; c_qact = E["c_qact"]
    c_n3a = E["c_n3a"]; c_nact = E["c_nact"]; c_b3a = E["c_b3a"]; c_inv = E["c_inv"]
    c_q3a = E["c_q3a"]; c_attn = E["c_attn"]; c_ope = E["c_ope"]; c_x1e = E["c_x1e"]
    c2_red = E["c2_red"]; c2_sq = E["c2_sq"]; c2_var = E["c2_var"]; c2_rstd = E["c2_rstd"]
    c2_negmr = E["c2_negmr"]; c_ln2 = E["c_ln2"]
    c_dv = E["c_dv"]; c_ac = E["c_ac"]
    eps_ln = E["eps_ln"]; eps_at = E["eps_at"]

    p3 = contextlib.ExitStack()
    sb = lambda *a: p3.enter_context(nc.sbuf_tensor(*a))
    pt = lambda *a: p3.enter_context(nc.psum_tensor(*a))
    wo_sb = sb("wo_sb", [P, 4 * FC * 512], BF16)
    x_sb3 = sb("x_sb3", [P, 2 * HID], F32)
    x1_sb = sb("x1_sb", [P, 2 * HID], F32)
    xn2t_sb = sb("xn2t_sb", [P, 2 * HID], BF16)
    sq_scr3 = sb("sq_scr3", [P, HID], F32)
    normb_sb = sb("normb_sb", [1, 1024], F32)
    invb_sb = sb("invb_sb", [P, 1024], F32)
    ps_n = pt("ps_n", [1, 1024], F32)
    ps_b = pt("ps_b", [P, 1024], F32)
    ps_qk = pt("ps_qk", [P, 1024], F32)
    ps_o = pt("ps_o", [P, 1024], F32)

    wov = wo_sb[:].rearrange("p (o k m) -> p o k m", o=4, k=FC)

    s_wol = E["s_wol"]

    @block.gpsimd
    def _(gp):
        gp.wait_ge(c_qev.h, 64)
        gp.wait_ge(c_qpe.h, 64)
        gp.wait_ge(s_cc.h, 33)
        for og in range(4):
            s_wol[og].inc(gp.dma_start(out=wo_sb[:, og * FC * 512:(og + 1) * FC * 512],
                                       in_=wo_blk[og, :, :]), 16)

    @block.tensor
    def _(t):
        t.wait_ge(c_kvbf.h, 1)
        for step in range(65):
            if step < 64:
                idx = step
                h, n = idx // 4, idx % 4
                if idx >= 2:
                    t.wait_ge(c_nact.h, idx - 1)
                c_n3a.inc(t.matmul(
                    ps_n[:, (idx % 2) * 512:(idx % 2 + 1) * 512],
                    ksum_bf[:, (h // 4):(h // 4) + 1],
                    sq_sb[:, h * T + n * 512:h * T + (n + 1) * 512],
                    start=True, stop=True))
            if step >= 1:
                idx = step - 1
                h, n = idx // 4, idx % 4
                t.wait_ge(c_nact.h, idx + 1)
                if idx >= 2:
                    t.wait_ge(c_inv.h, idx - 1)
                c_b3a.inc(t.matmul(
                    ps_b[:, (idx % 2) * 512:(idx % 2 + 1) * 512],
                    ones_row_f[:],
                    normb_sb[:, (idx % 2) * 512:(idx % 2 + 1) * 512],
                    start=True, stop=True))
                if idx >= 2:
                    t.wait_ge(c_attn.h, idx - 1)
                c_q3a.inc(t.matmul(
                    ps_qk[:, (idx % 2) * 512:(idx % 2 + 1) * 512],
                    kv_bf[:, (h // 4) * P:(h // 4 + 1) * P],
                    sq_sb[:, h * T + n * 512:h * T + (n + 1) * 512],
                    start=True, stop=True))
        for tc in range(TC):
            for og in range(4):
                oidx = tc * 4 + og
                if oidx >= 2:
                    t.wait_ge(c_x1e.h, oidx - 1)
                if oidx == 0:
                    t.wait_ge(c_attn.h, 64)
                if tc == 0:
                    t.wait_ge(s_wol[og].h, 16)
                ins = None
                for k in range(FC):
                    ins = t.matmul(
                        ps_o[:, (oidx % 2) * 512:(oidx % 2 + 1) * 512],
                        sq_sb[:, k * T + tc * P:k * T + tc * P + P],
                        wov[:, og, k, :],
                        start=(k == 0), stop=(k == FC - 1))
                c_ope.inc(ins)

    @block.sync
    def _(sync):
        def spill(tp):
            sync.wait_ge(c_ln2.h, tp + 1)
            s_spx[tp % 2].inc(sync.dma_start(out=x1_sp[tp * P:(tp + 1) * P, :],
                                             in_=x1_sb[:, (tp % 2) * HID:(tp % 2 + 1) * HID]), 16)
            s_spn[tp % 2].inc(sync.dma_start(out=xn2_sp[tp * P:(tp + 1) * P, :],
                                             in_=xn2t_sb[:, (tp % 2) * HID:(tp % 2 + 1) * HID]), 16)
        for tc in range(TC):
            if tc == 0:
                sync.wait_ge(c_qpe.h, 64)
                sync.wait_ge(c_kvbf.h, 1)
                sync.wait_ge(c_qact.h, 64)
            if tc >= 2:
                sync.wait_ge(c_x1e.h, 4 * tc - 4)
            s_x3[tc % 2].inc(sync.dma_start(
                out=x_sb3[:, (tc % 2) * HID:(tc % 2 + 1) * HID],
                in_=x_in[tc * P:(tc + 1) * P, :]), 16)
            if tc >= 2:
                spill(tc - 2)
        spill(TC - 2)
        spill(TC - 1)

    @block.scalar
    def _(s):
        for idx in range(64):
            s.wait_ge(c_n3a.h, idx + 1)
            if idx >= 2:
                s.wait_ge(c_b3a.h, idx - 1)
            c_nact.inc(s.activation(
                out=normb_sb[:, (idx % 2) * 512:(idx % 2 + 1) * 512],
                in_=ps_n[:, (idx % 2) * 512:(idx % 2 + 1) * 512],
                func=AF.Identity, bias=eps_at[0:1, :]))
        for tc in range(TC):
            par = tc % 2
            sc = par * 12
            x1t = x1_sb[:, par * HID:(par + 1) * HID]
            s.wait_ge(c2_red.h, tc + 1)
            c2_sq.inc(s.activation(out=sq_scr3[:], in_=x1t, func=AF.Square,
                                   accum_out=stat[:, sc + 1:sc + 2]))
            s.wait_ge(c2_var.h, tc + 1)
            c_ac.inc(s.activation(out=stat[:, sc + 6:sc + 7], in_=stat[:, sc + 5:sc + 6],
                                  func=AF.Ln, bias=eps_ln[:]))
            c_ac.barrier(s)
            c2_rstd.inc(s.activation(out=stat[:, sc + 7:sc + 8], in_=stat[:, sc + 6:sc + 7],
                                     func=AF.Exp, scale=-0.5))
            c2_rstd.barrier(s)
            s.wait_ge(c2_negmr.h, tc + 1)
            if tc >= 2:
                s.wait_ge(s_spn[tc % 2].h, 16 * ((tc - 2) // 2 + 1))
            c_ln2.inc(s.activation(out=xn2t_sb[:, par * HID:(par + 1) * HID], in_=x1t,
                                   func=AF.Identity, scale=stat[:, sc + 7:sc + 8],
                                   bias=stat[:, sc + 8:sc + 9]))

    @block.vector
    def _(v):
        for idx in range(64):
            h, n = idx // 4, idx % 4
            v.wait_ge(c_b3a.h, idx + 1)
            if idx >= 2:
                v.wait_ge(c_attn.h, idx - 1)
            c_inv.inc(v.reciprocal(out=invb_sb[:, (idx % 2) * 512:(idx % 2 + 1) * 512],
                                   in_=ps_b[:, (idx % 2) * 512:(idx % 2 + 1) * 512]))
            c_inv.barrier(v)
            v.wait_ge(c_q3a.h, idx + 1)
            c_attn.inc(v.tensor_tensor(
                out=sq_sb[:, h * T + n * 512:h * T + (n + 1) * 512],
                in0=ps_qk[:, (idx % 2) * 512:(idx % 2 + 1) * 512],
                in1=invb_sb[:, (idx % 2) * 512:(idx % 2 + 1) * 512], op=ALU.mult))
        for tc in range(TC):
            par = tc % 2
            sc = par * 12
            for og in range(4):
                oidx = tc * 4 + og
                v.wait_ge(c_ope.h, oidx + 1)
                v.wait_ge(s_x3[tc % 2].h, 16 * (tc // 2 + 1))
                if og == 0 and tc >= 2:
                    v.wait_ge(s_spx[tc % 2].h, 16 * ((tc - 2) // 2 + 1))
                c_x1e.inc(v.tensor_tensor(
                    out=x1_sb[:, par * HID + og * 512:par * HID + (og + 1) * 512],
                    in0=ps_o[:, (oidx % 2) * 512:(oidx % 2 + 1) * 512],
                    in1=x_sb3[:, par * HID + og * 512:par * HID + (og + 1) * 512],
                    op=ALU.add))
            x1t = x1_sb[:, par * HID:(par + 1) * HID]
            c_x1e.barrier(v)
            c2_red.inc(v.reduce_sum(out=stat[:, sc + 0:sc + 1], in_=x1t, axis=AX.X))
            c2_red.barrier(v)
            v.wait_ge(c2_sq.h, tc + 1)
            c_dv.inc(v.tensor_scalar(out=stat[:, sc + 2:sc + 3], in0=stat[:, sc + 0:sc + 1],
                                     scalar1=INV_N, scalar2=None, op0=ALU.mult))
            c_dv.barrier(v)
            c_dv.inc(v.tensor_tensor(out=stat[:, sc + 4:sc + 5], in0=stat[:, sc + 2:sc + 3],
                                     in1=stat[:, sc + 2:sc + 3], op=ALU.mult))
            c_dv.barrier(v)
            c2_var.inc(v.tensor_scalar(out=stat[:, sc + 5:sc + 6], in0=stat[:, sc + 1:sc + 2],
                                       scalar1=INV_N, scalar2=stat[:, sc + 4:sc + 5],
                                       op0=ALU.mult, op1=ALU.subtract))
            v.wait_ge(c2_rstd.h, tc + 1)
            c2_negmr.inc(v.tensor_scalar(out=stat[:, sc + 8:sc + 9], in0=stat[:, sc + 2:sc + 3],
                                         scalar1=stat[:, sc + 7:sc + 8], scalar2=-1.0,
                                         op0=ALU.mult, op1=ALU.mult))

    p3.close()


def _p4(nc, block, E):
    x1_sp = E["x1_sp"]; xn2_sp = E["xn2_sp"]; out = E["out"]
    wgu_blk = E["wgu_blk"]; wd_blk = E["wd_blk"]
    ident_bf = E["ident_bf"]; ident_f = E["ident_f"]
    s_wgu = E["s_wgu"]; s_wd = E["s_wd"]; s_x2 = E["s_x2"]; s_x1r = E["s_x1r"]
    s_out = E["s_out"]; s_spx = E["s_spx"]; s_spn = E["s_spn"]
    c_ope = E["c_ope"]; c_x1e = E["c_x1e"]; c_attn = E["c_attn"]; c_ln2 = E["c_ln2"]
    c_x2t = E["c_x2t"]; c_x2f = E["c_x2f"]
    c_gpe = E["c_gpe"]; c_upe = E["c_upe"]; c_sil = E["c_sil"]; c_guev = E["c_guev"]
    c_dpe = E["c_dpe"]; c_dev = E["c_dev"]; c_dwd = E["c_dwd"]; c_ddone = E["c_ddone"]
    c_ft = E["c_ft"]; c_fev = E["c_fev"]; c_fin = E["c_fin"]
    debug = E["debug"]

    p4 = contextlib.ExitStack()
    sb = lambda *a: p4.enter_context(nc.sbuf_tensor(*a))
    pt = lambda *a: p4.enter_context(nc.psum_tensor(*a))
    acc_sb = sb("acc_sb", [P, FC * TTG], F32)
    gated_sb = sb("gated_sb", [P, 16 * TTG], BF16)
    xn2f_sb = sb("xn2f_sb", [P, FC * TTG], BF16)
    wgu_sb = sb("wgu_sb", [P, 2 * FC * 256], BF16)
    wd_sb = sb("wd_sb", [P, 2 * FC * 256], BF16)
    x1_sb4 = sb("x1_sb4", [P, 2 * HID], F32)
    sg_tmp = sb("sg_tmp", [P, 1024], F32)
    ps_gu = pt("ps_gu", [P, 2048], F32)
    ps_tf = pt("ps_tf", [P, 2048], F32)

    accv = acc_sb[:].rearrange("p (f t) -> p f t", f=FC)
    gatv = gated_sb[:].rearrange("p (k t) -> p k t", k=16)
    xn2fv = xn2f_sb[:].rearrange("p (f t) -> p f t", f=FC)
    wguv = wgu_sb[:].rearrange("p (b k m) -> p b k m", b=2, k=FC)
    wdv = wd_sb[:].rearrange("p (b k m) -> p b k m", b=2, k=FC)

    @block.gpsimd
    def _(gp):
        gp.wait_ge(c_ope.h, 64)
        gp.wait_ge(c_x1e.h, 64)
        gp.wait_ge(c_ln2.h, 16)
        for tg in range(NTG):
            for r in range(NR):
                for hl in range(16):
                    hcb = r * 16 + hl
                    ci = tg * 64 + hcb
                    if ci >= 2:
                        gp.wait_ge(c_upe.h, 2 * (ci - 1))
                    s_wgu[ci % 2].inc(gp.dma_start(
                        out=wgu_sb[:, (ci % 2) * FC * 256:(ci % 2 + 1) * FC * 256],
                        in_=wgu_blk[hcb, :, :]), 16)
                for fg in range(FG):
                    wj = tg * 32 + r * 8 + fg
                    if wj >= 2:
                        gp.wait_ge(c_dpe.h, 4 * (wj - 1))
                    s_wd[wj % 2].inc(gp.dma_start(
                        out=wd_sb[:, (wj % 2) * FC * 256:(wj % 2 + 1) * FC * 256],
                        in_=wd_blk[r * 8 + fg, :, :]), 16)

    @block.sync
    def _(sync):
        for tg in range(NTG):
            pass
            for i in range(8):
                gi = tg * 8 + i
                if gi == 0:
                    for s_ in (s_spx[0], s_spx[1], s_spn[0], s_spn[1]):
                        sync.wait_ge(s_.h, s_.n)
                    sync.wait_ge(c_ope.h, 64)
                    sync.wait_ge(c_ln2.h, 16)
                if i >= 2:
                    gp_ = gi - 2
                    sync.wait_ge(c_fev.h, 4 * (gp_ + 1))
                    s_out[gp_ % 2].inc(sync.dma_start(
                        out=out[gp_ * P:(gp_ + 1) * P, :],
                        in_=x1_sb4[:, (gp_ % 2) * HID:(gp_ % 2 + 1) * HID]), 16)
                    pass
                if gi >= 2:
                    sync.wait_ge(s_out[gi % 2].h, 16 * ((gi - 2) // 2 + 1))
                s_x1r[gi % 2].inc(sync.dma_start(
                    out=x1_sb4[:, (gi % 2) * HID:(gi % 2 + 1) * HID],
                    in_=x1_sp[gi * P:(gi + 1) * P, :]), 16)
            for gp_ in (tg * 8 + 6, tg * 8 + 7):
                sync.wait_ge(c_fev.h, 4 * (gp_ + 1))
                s_out[gp_ % 2].inc(sync.dma_start(
                    out=out[gp_ * P:(gp_ + 1) * P, :],
                    in_=x1_sb4[:, (gp_ % 2) * HID:(gp_ % 2 + 1) * HID]), 16)

    @block.tensor
    def _(t):
        for tg in range(NTG):
            for r in range(NR):
                for hl in range(16):
                    hcb = r * 16 + hl
                    ci = tg * 64 + hcb
                    t.wait_ge(s_wgu[ci % 2].h, 16 * (ci // 2 + 1))
                    if hl == 0 and r == 0:
                        t.wait_ge(s_x2[0].h, 256 * (tg + 1))
                    if hl == 0:
                        t.wait_ge(c_dev.h, tg * 128 + r * 32)
                    for gh in range(2):
                        for n in range(2):
                            eidx = tg * 128 + hcb * 2 + n
                            if gh == 0 and eidx >= 2:
                                t.wait_ge(c_guev.h, eidx - 1)
                            ins = None
                            for k in range(FC):
                                ins = t.matmul(
                                    ps_gu[:, gh * 1024 + n * 512:gh * 1024 + (n + 1) * 512],
                                    wguv[:, ci % 2, k, gh * P:(gh + 1) * P],
                                    xn2fv[:, k, n * 512:(n + 1) * 512],
                                    start=(k == 0), stop=(k == FC - 1))
                            if gh == 0:
                                c_gpe.inc(ins)
                            else:
                                c_upe.inc(ins)
                t.wait_ge(c_guev.h, tg * 128 + (r + 1) * 32)
                for fg in range(FG):
                    wj = tg * 32 + r * 8 + fg
                    t.wait_ge(s_wd[wj % 2].h, 16 * (wj // 2 + 1))
                    ins = None
                    for mc in range(2):
                        for n in range(2):
                            didx = tg * 128 + r * 32 + fg * 4 + mc * 2 + n
                            if didx >= 2:
                                t.wait_ge(c_dev.h, didx - 1)
                            for k in range(FC):
                                ins = t.matmul(
                                    ps_gu[:, (didx % 2) * 512:(didx % 2 + 1) * 512],
                                    wdv[:, wj % 2, k, mc * P:(mc + 1) * P],
                                    gatv[:, k, n * 512:(n + 1) * 512],
                                    start=(k == 0), stop=(k == FC - 1))
                            c_dpe.inc(ins)
            for i in range(8):
                gi = tg * 8 + i
                if i == 0:
                    t.wait_ge(c_dev.h, (tg + 1) * 128)
                for gr in range(4):
                    G = gi * 4 + gr
                    if G >= 4:
                        t.wait_ge(c_fev.h, G - 3)
                    ins = None
                    for j in range(4):
                        fc = gr * 4 + j
                        ins = t.transpose(
                            ps_tf[:, (G % 4) * 512 + j * P:(G % 4) * 512 + (j + 1) * P],
                            accv[:, fc, i * P:(i + 1) * P],
                            ident_f[:])
                    c_ft.inc(ins)

    @block.scalar
    def _(s):
        for tg in range(NTG):
            if tg == 0:
                for s_ in (s_spx[0], s_spx[1], s_spn[0], s_spn[1]):
                    s.wait_ge(s_.h, s_.n)
                s.wait_ge(c_ope.h, 64)
                s.wait_ge(c_ln2.h, 16)
            else:
                s.wait_ge(c_upe.h, 128)
            for fc in range(FC):
                s_x2[0].inc(s.dma_start(
                    out=xn2fv[:, fc, :],
                    in_=xn2_sp[tg * TTG:(tg + 1) * TTG, fc * P:(fc + 1) * P],
                    transpose=True), 16)
            for hcb in range(64):
                for n in range(2):
                    eidx = tg * 128 + hcb * 2 + n
                    s.wait_ge(c_gpe.h, eidx + 1)
                    if eidx >= 2:
                        s.wait_ge(c_guev.h, eidx - 1)
                    c_sil.inc(s.activation(
                        out=sg_tmp[:, n * 512:(n + 1) * 512],
                        in_=ps_gu[:, n * 512:(n + 1) * 512], func=AF.Silu))

    @block.vector
    def _(v):
        for tg in range(NTG):
            for r in range(NR):
                for hl in range(16):
                    hcb = r * 16 + hl
                    for n in range(2):
                        eidx = tg * 128 + hcb * 2 + n
                        v.wait_ge(c_sil.h, eidx + 1)
                        v.wait_ge(c_upe.h, eidx + 1)
                        if tg * 4 + r >= 1 and hl == 0 and n == 0:
                            v.wait_ge(c_dpe.h, tg * 128 + r * 32)
                        c_guev.inc(v.tensor_tensor(
                            out=gatv[:, hl, n * 512:(n + 1) * 512],
                            in0=sg_tmp[:, n * 512:(n + 1) * 512],
                            in1=ps_gu[:, 1024 + n * 512:1024 + (n + 1) * 512],
                            op=ALU.mult))
                for fg in range(FG):
                    for mc in range(2):
                        for n in range(2):
                            didx = tg * 128 + r * 32 + fg * 4 + mc * 2 + n
                            v.wait_ge(c_dpe.h, didx + 1)
                            fc = fg * 2 + mc
                            if r == 0:
                                if tg == 1 and fg == 0 and mc == 0 and n == 0:
                                    v.wait_ge(c_ft.h, 32)
                                c_dev.inc(v.tensor_copy(
                                    out=accv[:, fc, n * 512:(n + 1) * 512],
                                    in_=ps_gu[:, (didx % 2) * 512:(didx % 2 + 1) * 512]))
                            else:
                                c_dev.inc(v.tensor_tensor(
                                    out=accv[:, fc, n * 512:(n + 1) * 512],
                                    in0=ps_gu[:, (didx % 2) * 512:(didx % 2 + 1) * 512],
                                    in1=accv[:, fc, n * 512:(n + 1) * 512],
                                    op=ALU.add))
            for i in range(8):
                gi = tg * 8 + i
                v.wait_ge(s_x1r[gi % 2].h, 16 * (gi // 2 + 1))
                for gr in range(4):
                    G = gi * 4 + gr
                    v.wait_ge(c_ft.h, G + 1)
                    ins = v.tensor_tensor(
                        out=x1_sb4[:, (gi % 2) * HID + gr * 512:(gi % 2) * HID + (gr + 1) * 512],
                        in0=ps_tf[:, (G % 4) * 512:(G % 4 + 1) * 512],
                        in1=x1_sb4[:, (gi % 2) * HID + gr * 512:(gi % 2) * HID + (gr + 1) * 512],
                        op=ALU.add)
                    c_fev.inc(ins)

    p4.close()


def _prep(inputs):
    f32 = np.float32
    hs = np.asarray(inputs["L0_attn_hidden_states"], f32)
    Wq = np.asarray(inputs["L0_attn_Wq"], f32)
    bq = np.asarray(inputs["L0_attn_bq"], f32)
    Wk = np.asarray(inputs["L0_attn_Wk"], f32)
    bk = np.asarray(inputs["L0_attn_bk"], f32)
    Wv = np.asarray(inputs["L0_attn_Wv"], f32)
    bv = np.asarray(inputs["L0_attn_bv"], f32)
    Wo = np.asarray(inputs["L0_attn_Wo"], f32)
    Wg = np.asarray(inputs["L0_ffn_w_gate"], f32)
    Wu = np.asarray(inputs["L0_ffn_w_up"], f32)
    Wd = np.asarray(inputs["L0_ffn_w_down"], f32)

    wkv_t = np.concatenate([Wk, Wv], axis=0).T
    wkv_blk = np.ascontiguousarray(
        wkv_t.reshape(FC, P, 2 * KV).transpose(1, 0, 2).reshape(P, FC * 2 * KV))
    wq_blk = np.ascontiguousarray(
        Wq.T.reshape(FC, P, 8, 256).transpose(2, 1, 0, 3).reshape(8, P, FC * 256))
    wo_blk = np.ascontiguousarray(
        Wo.T.reshape(FC, P, 4, 512).transpose(2, 1, 0, 3).reshape(4, P, FC * 512))
    Gt = Wg.T.reshape(FC, P, 64, 128)
    Ut = Wu.T.reshape(FC, P, 64, 128)
    gu = np.concatenate([Gt, Ut], axis=3)
    wgu_blk = np.ascontiguousarray(gu.transpose(2, 1, 0, 3).reshape(64, P, FC * 256))
    WdT = np.ascontiguousarray(Wd.T)
    a = WdT.reshape(NR, FC, P, FG, 256).transpose(0, 3, 2, 1, 4)
    wd_blk = np.ascontiguousarray(a.reshape(NR * FG, P, FC * 256))

    bq_tile = np.ascontiguousarray(bq.reshape(FC, P).T)
    bkv_row = np.ascontiguousarray(np.concatenate([bk, bv]).reshape(1, 2 * KV))
    idb = np.eye(P, dtype=ml_dtypes.bfloat16)
    idf = np.eye(P, dtype=f32)
    sel8 = np.zeros((8, 1024), dtype=ml_dtypes.bfloat16)
    for j in range(8):
        sel8[j, j * 128:(j + 1) * 128] = 1

    common = dict(wkv_blk=wkv_blk, wq_blk=wq_blk, wo_blk=wo_blk, wgu_blk=wgu_blk,
                  wd_blk=wd_blk, bq_t=bq_tile, bkv_r=bkv_row, id_bf=idb, id_f32=idf,
                  sel8_d=sel8)
    in_maps = []
    for c in range(8):
        b, r0 = c // 4, (c % 4) * T
        m = dict(common)
        m["x_in"] = np.ascontiguousarray(hs[b, r0:r0 + T, :])
        in_maps.append(m)
    return in_maps


def kernel(**inputs):
    in_maps = _prep(inputs)
    nc, st = build_nc(debug=False)
    try:
        res = run_bass_kernel_spmd(nc, in_maps, core_ids=list(range(8)))
    finally:
        st.close()
    outp = np.empty((2, 8192, HID), np.float32)
    for c in range(8):
        b, r0 = c // 4, (c % 4) * T
        outp[b, r0:r0 + T, :] = res.results[c]["out"]
    return outp
